# version 22
# baseline (speedup 1.0000x reference)
import numpy as np
from contextlib import ExitStack

import concourse.bass as bass
import concourse.mybir as mybir
from concourse.bass_utils import run_bass_kernel_spmd

F32 = mybir.dt.float32
BF16 = mybir.dt.bfloat16
F8 = mybir.dt.float8e4
I32 = mybir.dt.int32
AF = mybir.ActivationFunctionType
ALU = mybir.AluOpType
AX = mybir.AxisListType

D = 1024
E = 2048
NS = 16
RDT = 64
KC = 4
INNER = 4096
H = 16
DH = 64
NE = E // 128
NSEQ_FULL = 4096


def build(T=NSEQ_FULL, phases=("M", "F1", "A", "F2"), debug_out=None, a_stages=2):
    NT = T // 128
    NCH = T // 512
    nc = bass.Bass("TRN2", target_bir_lowering=False)
    es = ExitStack()

    def din(name, shape, dt=F32):
        return nc.dram_tensor(name, list(shape), dt, kind="ExternalInput").ap()

    x0 = din("x0", (T, D))
    norm_w = din("norm_w", (D,))
    W_in = din("W_in", (D, 2 * E))
    conv_w = din("conv_w", (E, KC))
    W_x = din("W_x", (E, RDT + 2 * NS))
    W_dt = din("W_dt", (RDT, E))
    b_dt = din("b_dt", (E,))
    A_log = din("A_log", (E, NS))
    D_param = din("D_param", (E,))
    W_out = din("W_out", (E, D))
    WqT = din("WqT", (D, D))
    bq = din("bq", (D,))
    WkT = din("WkT", (2, D, D))
    bk = din("bk", (2, D))
    WvT = din("WvT", (2, D, D))
    bv = din("bv", (2, D))
    WoT = din("WoT", (D, D))
    bo = din("bo", (D,))
    ln_g = din("ln_g", (D,))
    ln_b = din("ln_b", (D,))
    wgT = din("wgT", (D, INNER))
    wuT = din("wuT", (D, INNER))
    wdT = din("wdT", (INNER, D))

    y_out = nc.dram_tensor("y", [T, D], F32, kind="ExternalOutput").ap()

    def scr(name, shape, dt):
        return nc.dram_tensor(name, list(shape), dt, kind="Internal").ap()

    xs1 = scr("xs1", (T, D), F32)
    xs2 = scr("xs2", (T, D), F32)
    xs3 = scr("xs3", (T, D), F32)
    qn_hbm = scr("qn_hbm", (T, D), F32)
    z_hbm = scr("z_hbm", (E, T), BF16)
    wbin_t = scr("wbin_t", (2 * NE, 128, 8 * 128), BF16)
    wbout_t = scr("wbout_t", (2, NE, 128, 512), BF16)
    wat_t = scr("wat_t", (5, 2, 8, 128, 512), BF16)
    wo_t = scr("wo_t", (2, 8, 128, 512), BF16)
    wg8 = scr("wg8", (8, 8, 128, 512), F8)
    wu8 = scr("wu8", (8, 8, 128, 512), F8)
    wd8 = scr("wd8", (32, 2, 128, 512), F8)

    def psb(name, shape, dt=F32):
        return es.enter_context(nc.sbuf_tensor(name, list(shape), dt))

    ident_f = psb("ident_f", (128, 128))
    ident_b = psb("ident_b", (128, 128), BF16)
    ones_b = psb("ones_b", (1, 128), BF16)
    ones_col = psb("ones_col", (128, 1))
    ones_f = psb("ones_f", (1, 128))
    cbias = psb("cbias", (128, 3))
    A_sb = psb("A_sb", (128, NE * NS))
    conv_sb = psb("conv_sb", (128, NE * KC))
    D_sb = psb("D_sb", (128, NE))
    bdt_sb = psb("bdt_sb", (128, NE))
    norm_sb = psb("norm_sb", (128, 8))
    battn = psb("battn", (1, 5 * D), BF16)
    bwo = psb("bwo", (1, D), BF16)
    rswg = psb("rswg", (128, 4))
    swcol = psb("swcol", (128, 4))
    stat = psb("stat", (128, 64))

    all_sems = []

    def mksem(name):
        class _S:
            pass
        s = _S()
        s.h = es.enter_context(nc.semaphore(name))
        s.n = 0
        all_sems.append(s)
        return s

    sv = mksem("sv")
    ss = mksem("ss")
    st = mksem("st")
    sg = mksem("sg")
    sl = mksem("sl")
    sw = mksem("sw")
    sl2 = mksem("sl2")

    def mkring(name, k):
        return [mksem(name + str(i)) for i in range(k)]

    r_mxl = mkring("r_mxl", 2)
    r_win = mkring("r_win", 2)
    r_wou = mkring("r_wou", 2)
    r_zrl = mkring("r_zrl", 4)
    r_mxr = mkring("r_mxr", 1)
    r_zbc = mkring("r_zbc", 2)
    r_zst = mkring("r_zst", 2)
    r_fxl = mkring("r_fxl", 2)
    r_fwd = mkring("r_fwd", 2)
    r_axl = mkring("r_axl", 2)
    r_aqn = mkring("r_aqn", 1)
    s_wt = mksem("s_wt")
    s_wt2 = mksem("s_wt2")
    sw1 = mksem("sw1")
    s_dst = mksem("s_dst")
    s_cst = mksem("s_cst")
    s_woa = mksem("s_woa")
    s_lng = mksem("s_lng")

    cur = [None]

    def _emit(engine, make, inc, k, waits):
        best = {}
        for wv in waits:
            if wv is None:
                continue
            h, v = wv
            if v > 0 and v > best.get(id(h), (None, -1))[1]:
                best[id(h)] = (h, v)
        waits = list(best.values())
        if inc is not None:
            inc.n += k
        sem = inc.h if inc is not None else None

        def fn(eng, waits=waits, make=make, sem=sem, k=k):
            for h, v in waits:
                eng.wait_ge(h, v)
            instr = make(eng)
            if sem is not None:
                instr.then_inc(sem, k)
        cur[0].emit(engine, fn)
        return (inc.h, inc.n) if inc is not None else None

    def SY(make, inc=sl, k=16, waits=()):
        return _emit("sync", make, inc, k, waits)

    def SD(make, inc=sl2, k=16, waits=()):
        return _emit("scalar", make, inc, k, waits)

    def GP(make, inc=sg, k=1, waits=()):
        return _emit("gpsimd", make, inc, k, waits)

    def VE(make, inc=sv, k=1, waits=()):
        return _emit("vector", make, inc, k, waits)

    def SC(make, inc=ss, k=1, waits=()):
        return _emit("scalar", make, inc, k, waits)

    def TE(make, inc=st, k=1, waits=()):
        return _emit("tensor", make, inc, k, waits)

    def W(*sems):
        return [(s.h, s.n) for s in sems]

    def MK(s):
        return (s.h, s.n)

    class Prog:
        ENGINES = ("sync", "vector", "scalar", "tensor", "gpsimd")

        def __init__(self):
            self.streams = {e: [] for e in self.ENGINES}

        def emit(self, engine, fn):
            self.streams[engine].append(fn)

        def barrier(self):
            waits = W(*all_sems)
            for e in self.ENGINES:
                def fn(eng, waits=waits):
                    for h, v in waits:
                        if v > 0:
                            eng.wait_ge(h, v)
                self.emit(e, fn)

        def run(self):
            with nc.Block() as blk:
                @blk.sync
                def _(eng, fns=self.streams["sync"]):
                    for f in fns:
                        f(eng)

                @blk.tensor
                def _(eng, fns=self.streams["tensor"]):
                    for f in fns:
                        f(eng)

                @blk.scalar
                def _(eng, fns=self.streams["scalar"]):
                    for f in fns:
                        f(eng)

                @blk.vector
                def _(eng, fns=self.streams["vector"]):
                    for f in fns:
                        f(eng)

                @blk.gpsimd
                def _(eng, fns=self.streams["gpsimd"]):
                    for f in fns:
                        f(eng)

    def new_prog():
        p = Prog()
        cur[0] = p
        p.barrier()
        return p

    def ln_stats(src_ap, dump_ap, cs, cq, cm_, cv, rdy=None):
        rw = W(rdy) if rdy is not None else []
        SC(lambda s: s.activation(out=dump_ap, in_=src_ap, func=AF.Identity,
                                  accum_out=cs), waits=rw + W(ss, sv))
        SC(lambda s: s.activation(out=dump_ap, in_=src_ap, func=AF.Square,
                                  accum_out=cq), waits=W(ss))
        VE(lambda v: v.tensor_scalar(out=cm_, in0=cs, scalar1=1.0 / D,
                                     scalar2=None, op0=ALU.mult), waits=W(ss, sv))
        VE(lambda v: v.tensor_scalar(out=cv, in0=cq, scalar1=1.0 / D,
                                     scalar2=None, op0=ALU.mult), waits=W(sv))
        tmpc = dump_ap[:, 0:1]
        VE(lambda v: v.tensor_tensor(out=tmpc, in0=cm_, in1=cm_, op=ALU.mult),
           waits=W(sv, ss))
        VE(lambda v: v.tensor_tensor(out=cv, in0=cv, in1=tmpc, op=ALU.subtract),
           waits=W(sv))

    def rsqrt_eps(cv, eps, cq, crq, crs):
        VE(lambda v: v.tensor_scalar(out=cq, in0=cv, scalar1=eps, scalar2=None,
                                     op0=ALU.add), waits=W(sv))
        VE(lambda v: v.reciprocal(out=crq, in_=cq), waits=W(sv))
        SC(lambda s: s.activation(out=crs, in_=crq, func=AF.Sqrt), waits=W(sv, ss))

    prep = new_prog()
    with nc.sbuf_tensor("p_itmp", [128, 128], F32) as itmp, \
         nc.sbuf_tensor("p_icol", [128, 1], F32) as icol, \
         nc.sbuf_tensor("p_wt", [128, 8192], F32) as wt, \
         nc.sbuf_tensor("p_wb", [128, 8192], BF16) as wb, \
         nc.sbuf_tensor("p_wi", [128, 8192], F32) as wi, \
         nc.sbuf_tensor("p_w8", [128, 8192], F8) as w8, \
         nc.sbuf_tensor("p_col", [128, 4], F32) as pcol, \
         nc.sbuf_tensor("p_row", [1, 16], F32) as prow, \
         nc.sbuf_tensor("p_brow", [1, 5 * D], F32) as brow, \
         nc.sbuf_tensor("p_lnb", [128, 8], F32) as lnbc, \
         nc.sbuf_tensor("p_bo", [1, D], F32) as borow, \
         nc.sbuf_tensor("p_wo", [128, 16 * 512], F32) as wochunk, \
         nc.sbuf_tensor("p_bwo", [1, D], F32) as bwof, \
         nc.psum_tensor("p_ps", [128, 512], F32) as pps:

        GP(lambda g: g.iota(itmp[:, :], [[1, 128]], channel_multiplier=0,
                            allow_small_or_imprecise_dtypes=True))
        GP(lambda g: g.iota(icol[:, :], [[0, 1]], channel_multiplier=1,
                            allow_small_or_imprecise_dtypes=True))
        GP(lambda g: g.memset(ones_b[:, :], 1.0))
        GP(lambda g: g.memset(ones_col[:, :], 1.0))
        GP(lambda g: g.memset(ones_f[:, :], 1.0))
        GP(lambda g: g.memset(cbias[:, 0:1], 12582912.0))
        GP(lambda g: g.memset(cbias[:, 1:2], -12582912.0))
        GP(lambda g: g.memset(cbias[:, 2:3], 1.0))
        VE(lambda v: v.tensor_scalar(out=ident_f[:, :], in0=itmp[:, :],
                                     scalar1=icol[:, :], scalar2=None,
                                     op0=ALU.is_equal), waits=W(sg))
        VE(lambda v: v.tensor_copy(out=ident_b[:, :], in_=ident_f[:, :]),
           waits=W(sv))

        for et in range(NE):
            SY(lambda s, et=et: s.dma_start(out=A_sb[:, et * NS:(et + 1) * NS],
                                            in_=A_log[et * 128:(et + 1) * 128, :]),
               inc=s_cst)
            SY(lambda s, et=et: s.dma_start(out=conv_sb[:, et * KC:(et + 1) * KC],
                                            in_=conv_w[et * 128:(et + 1) * 128, :]),
               inc=s_cst)
            SY(lambda s, et=et: s.dma_start(
                out=D_sb[:, et:et + 1], in_=D_param[et * 128:(et + 1) * 128][:, None]),
               inc=s_cst)
            SY(lambda s, et=et: s.dma_start(
                out=bdt_sb[:, et:et + 1], in_=b_dt[et * 128:(et + 1) * 128][:, None]),
               inc=s_cst)
        for cc in range(8):
            SY(lambda s, cc=cc: s.dma_start(
                out=norm_sb[:, cc:cc + 1], in_=norm_w[cc * 128:(cc + 1) * 128][:, None]),
               inc=s_cst)
        SY(lambda s: s.dma_start(out=brow[0:1, 0:D], in_=bq[:][None, :]), inc=s_cst)
        SY(lambda s: s.dma_start(out=brow[0:1, D:2 * D], in_=bk[0, :][None, :]), inc=s_cst)
        SY(lambda s: s.dma_start(out=brow[0:1, 2 * D:3 * D], in_=bk[1, :][None, :]), inc=s_cst)
        SY(lambda s: s.dma_start(out=brow[0:1, 3 * D:4 * D], in_=bv[0, :][None, :]), inc=s_cst)
        SY(lambda s: s.dma_start(out=brow[0:1, 4 * D:5 * D], in_=bv[1, :][None, :]), inc=s_cst)
        for cc in range(8):
            SY(lambda s, cc=cc: s.dma_start(
                out=lnbc[:, cc:cc + 1], in_=ln_b[cc * 128:(cc + 1) * 128][:, None]),
               inc=s_cst)
        SY(lambda s: s.dma_start(out=borow[:, :], in_=bo[:][None, :]), inc=s_cst)
        for half in range(2):
            for cc in range(8):
                SD(lambda s, cc=cc, half=half: s.dma_start(
                    out=wochunk[:, (half * 8 + cc) * 512:(half * 8 + cc + 1) * 512],
                    in_=WoT[cc * 128:(cc + 1) * 128, half * 512:(half + 1) * 512]),
                   inc=s_cst)
        SC(lambda s: s.activation(out=A_sb[:, :], in_=A_sb[:, :], func=AF.Exp),
           waits=W(s_cst))
        VE(lambda v: v.tensor_scalar(out=A_sb[:, :], in0=A_sb[:, :], scalar1=-1.0,
                                     scalar2=None, op0=ALU.mult), waits=W(ss))
        VE(lambda v: v.tensor_copy(out=battn[:, :], in_=brow[:, :]), waits=W(s_cst))
        for half in range(2):
            for cc in range(8):
                TE(lambda t, cc=cc, half=half: t.matmul(
                    pps[0:1, 0:512], lnbc[:, cc:cc + 1],
                    wochunk[:, (half * 8 + cc) * 512:(half * 8 + cc + 1) * 512],
                    start=(cc == 0), stop=(cc == 7)),
                   inc=st if cc == 7 else None,
                   waits=(W(s_cst, sv) if cc == 0 else ()))
            VE(lambda v, half=half: v.tensor_tensor(
                out=bwof[0:1, half * 512:(half + 1) * 512], in0=pps[0:1, 0:512],
                in1=borow[0:1, half * 512:(half + 1) * 512], op=ALU.add),
               waits=W(st, s_cst, sv))
        VE(lambda v: v.tensor_copy(out=bwo[:, :], in_=bwof[:, :]), waits=W(sv))

        WTW = 4096
        wt_guard = {0: None, 1: None}
        out_guard = {0: None, 1: None}
        rctr = [0]

        def ld_row(src_ap, r0, width):
            slq = rctr[0] % 2
            rctr[0] += 1
            Q = SY if slq == 0 else SD
            ld = Q(lambda s, src_ap=src_ap, r0=r0, width=width, slq=slq:
                   s.dma_start(out=wt[:, slq * WTW:slq * WTW + width],
                               in_=src_ap[r0:r0 + 128, 0:width]),
                   inc=(s_wt if slq == 0 else s_wt2), waits=[wt_guard[slq]])
            return slq, ld

        def retile(src_ap, n_cc, width, dst_fn, add_i=False):
            for cc in range(n_cc):
                slq, ld = ld_row(src_ap, cc * 128, width)
                prev = ld
                if add_i:
                    prev = VE(lambda v, cc=cc, slq=slq: v.tensor_tensor(
                        out=wt[:, slq * WTW + cc * 128:slq * WTW + (cc + 1) * 128],
                        in0=wt[:, slq * WTW + cc * 128:slq * WTW + (cc + 1) * 128],
                        in1=ident_f[:, :], op=ALU.add), waits=[ld] + W(sv))
                cp = VE(lambda v, slq=slq, width=width: v.tensor_copy(
                    out=wb[:, slq * WTW:slq * WTW + width],
                    in_=wt[:, slq * WTW:slq * WTW + width]),
                    waits=[prev, out_guard[slq]] + W(sv))
                wt_guard[slq] = cp
                out_guard[slq] = dst_fn(cc, slq, [cp])

        def win_dst(cc, slq, wl):
            mk = None
            for et in range(2 * NE):
                mk = GP(lambda g, cc=cc, et=et, slq=slq: g.dma_start(
                    out=wbin_t[et, :, cc * 128:(cc + 1) * 128],
                    in_=wb[:, slq * WTW + et * 128:slq * WTW + (et + 1) * 128]),
                    inc=(sw if slq == 0 else sw1), k=16, waits=wl)
            return mk
        retile(W_in, 8, 2 * E, win_dst)

        def wout_dst(et, slq, wl):
            mk = None
            for half in range(2):
                mk = GP(lambda g, et=et, half=half, slq=slq: g.dma_start(
                    out=wbout_t[half, et, :, :],
                    in_=wb[:, slq * WTW + half * 512:slq * WTW + (half + 1) * 512]),
                    inc=(sw if slq == 0 else sw1), k=16, waits=wl)
            return mk
        retile(W_out, NE, D, wout_dst)

        for pi, srcw in enumerate([WqT, WkT[0], WkT[1], WvT[0], WvT[1]]):
            def att_dst(cc, slq, wl, pi=pi):
                mk = None
                for half in range(2):
                    mk = GP(lambda g, cc=cc, half=half, pi=pi, slq=slq: g.dma_start(
                        out=wat_t[pi, half, cc, :, :],
                        in_=wb[:, slq * WTW + half * 512:slq * WTW + (half + 1) * 512]),
                        inc=(sw if slq == 0 else sw1), k=16, waits=wl)
                return mk
            retile(srcw, 8, D, att_dst, add_i=True)

        for cc in range(8):
            slq = rctr[0] % 2
            rctr[0] += 1
            c1 = VE(lambda v, cc=cc, slq=slq: v.tensor_copy(
                out=wb[:, slq * WTW:slq * WTW + 512],
                in_=wochunk[:, cc * 512:(cc + 1) * 512]),
                waits=[out_guard[slq]] + W(s_cst, sv))
            c2 = VE(lambda v, cc=cc, slq=slq: v.tensor_copy(
                out=wb[:, slq * WTW + 512:slq * WTW + 1024],
                in_=wochunk[:, (8 + cc) * 512:(9 + cc) * 512]), waits=W(sv))
            if wt_guard[slq] is None:
                wt_guard[slq] = c2
            mk = None
            for half in range(2):
                mk = GP(lambda g, cc=cc, half=half, slq=slq: g.dma_start(
                    out=wo_t[half, cc, :, :],
                    in_=wb[:, slq * WTW + half * 512:slq * WTW + (half + 1) * 512]),
                    inc=(sw if slq == 0 else sw1), k=16, waits=[c2])
            out_guard[slq] = mk

        def quant(src_ap, n_cc, width, colidx, dst_fn):
            VE(lambda v: v.memset(pcol[:, 0:1], 0.0), waits=W(sv, st))
            for cc in range(n_cc):
                slq, ld = ld_row(src_ap, cc * 128, width)
                rd = VE(lambda v, slq=slq, width=width: v.tensor_reduce(
                    out=pcol[:, 1:2], in_=wt[:, slq * WTW:slq * WTW + width],
                    axis=AX.X, op=ALU.add, apply_absolute_value=True),
                    waits=[ld] + W(sv))
                wt_guard[slq] = rd
                VE(lambda v: v.tensor_tensor(out=pcol[:, 0:1], in0=pcol[:, 0:1],
                                             in1=pcol[:, 1:2], op=ALU.add),
                   waits=W(sv))
            TE(lambda t: t.matmul(pps[0:1, 0:1], pcol[:, 0:1], ones_col[:, :],
                                  start=True, stop=True), waits=W(sv, st, sg))
            VE(lambda v, n_cc=n_cc, width=width: v.tensor_scalar(
                out=prow[0:1, 0:1], in0=pps[0:1, 0:1],
                scalar1=1.0 / (n_cc * 128 * width),
                scalar2=1e-5, op0=ALU.mult, op1=ALU.max), waits=W(st, sv))
            VE(lambda v: v.reciprocal(out=prow[0:1, 1:2], in_=prow[0:1, 0:1]),
               waits=W(sv))
            TE(lambda t: t.matmul(pps[:, 0:2], ones_f[:, :], prow[0:1, 0:2],
                                  start=True, stop=True), waits=W(sv, st, sg))
            VE(lambda v, colidx=colidx: v.tensor_copy(
                out=rswg[:, colidx:colidx + 1], in_=pps[:, 0:1]), waits=W(st, sv))
            VE(lambda v, colidx=colidx: v.tensor_copy(
                out=swcol[:, colidx:colidx + 1], in_=pps[:, 1:2]), waits=W(st, sv))
            for cc in range(n_cc):
                slq, ld = ld_row(src_ap, cc * 128, width)
                q1 = VE(lambda v, colidx=colidx, slq=slq, width=width:
                        v.tensor_scalar(
                    out=wi[:, slq * WTW:slq * WTW + width],
                    in0=wt[:, slq * WTW:slq * WTW + width],
                    scalar1=swcol[:, colidx:colidx + 1], scalar2=12582912.0,
                    op0=ALU.mult, op1=ALU.add), waits=[ld] + W(sv))
                wt_guard[slq] = q1
                VE(lambda v, slq=slq, width=width: v.tensor_scalar(
                    out=wi[:, slq * WTW:slq * WTW + width],
                    in0=wi[:, slq * WTW:slq * WTW + width],
                    scalar1=-12582912.0, scalar2=1.0,
                    op0=ALU.add, op1=ALU.min), waits=W(sv))
                q3 = VE(lambda v, slq=slq, width=width: v.tensor_scalar(
                    out=w8[:, slq * WTW:slq * WTW + width],
                    in0=wi[:, slq * WTW:slq * WTW + width],
                    scalar1=-1.0, scalar2=None,
                    op0=ALU.max), waits=[out_guard[slq]] + W(sv))
                out_guard[slq] = dst_fn(cc, slq, [q3])

        def wg_dst(cc, slq, wl):
            mk = None
            for ob in range(8):
                mk = GP(lambda g, cc=cc, ob=ob, slq=slq: g.dma_start(
                    out=wg8[cc, ob, :, :],
                    in_=w8[:, slq * WTW + ob * 512:slq * WTW + (ob + 1) * 512]),
                    inc=(sw if slq == 0 else sw1), k=16, waits=wl)
            return mk
        quant(wgT, 8, INNER, 0, wg_dst)

        def wu_dst(cc, slq, wl):
            mk = None
            for ob in range(8):
                mk = GP(lambda g, cc=cc, ob=ob, slq=slq: g.dma_start(
                    out=wu8[cc, ob, :, :],
                    in_=w8[:, slq * WTW + ob * 512:slq * WTW + (ob + 1) * 512]),
                    inc=(sw if slq == 0 else sw1), k=16, waits=wl)
            return mk
        quant(wuT, 8, INNER, 1, wu_dst)

        def wd_dst(oc, slq, wl):
            mk = None
            for half in range(2):
                mk = GP(lambda g, oc=oc, half=half, slq=slq: g.dma_start(
                    out=wd8[oc, half, :, :],
                    in_=w8[:, slq * WTW + half * 512:slq * WTW + (half + 1) * 512]),
                    inc=(sw if slq == 0 else sw1), k=16, waits=wl)
            return mk
        quant(wdT, 32, D, 2, wd_dst)

        prep.run()

    if "M" in phases:
        pm = new_prog()
        with ExitStack() as esp:
            AL = lambda k, n, s, d: esp.enter_context(getattr(nc, k)(n, list(s), d))
            wx_bf = AL("sbuf_tensor", "m_wxb", [128, NE * 96], BF16)
            wdt_sb = AL("sbuf_tensor", "m_wdt", [RDT, E], F32)
            m_xt = AL("sbuf_tensor", "m_xt", [128, 2 * D], F32)
            m_xn = AL("sbuf_tensor", "m_xn", [128, D], F32)
            m_xnT = AL("sbuf_tensor", "m_xnT", [128, 8 * 512], BF16)
            m_win = AL("sbuf_tensor", "m_win", [128, 2 * 1024], BF16)
            m_stage = AL("sbuf_tensor", "m_stage", [128, 2 * 516], F32)
            m_cv = AL("sbuf_tensor", "m_cv", [128, 2 * 512], F32)
            m_u = AL("sbuf_tensor", "m_u", [128, 2 * NE * 512], BF16)
            m_szs = AL("sbuf_tensor", "m_szs", [128, 2 * 512], BF16)
            m_szl = AL("sbuf_tensor", "m_szl", [128, 4 * 512], BF16)
            m_x96 = AL("sbuf_tensor", "m_x96", [96, 2 * 512], F32)
            m_bcr = AL("sbuf_tensor", "m_bcr", [1, 1024], F32)
            m_bcb = AL("sbuf_tensor", "m_bcb", [1, 1024], BF16)
            m_Bb = AL("sbuf_tensor", "m_Bb", [128, NS * 512], BF16)
            m_Cb = AL("sbuf_tensor", "m_Cb", [128, NS * 512], BF16)
            m_delta = AL("sbuf_tensor", "m_delta", [128, 2 * 512], F32)
            m_sp = AL("sbuf_tensor", "m_sp", [128, 2 * 512], F32)
            m_du = AL("sbuf_tensor", "m_du", [128, 2 * 512], BF16)
            m_dA = AL("sbuf_tensor", "m_dA", [128, 4 * 512], F32)
            m_xx = AL("sbuf_tensor", "m_xx", [128, 2 * 2048], BF16)
            m_h = AL("sbuf_tensor", "m_h", [128, NS * 512], BF16)
            m_hcar = AL("sbuf_tensor", "m_hcar", [128, NE * NS], F32)
            m_ccar = AL("sbuf_tensor", "m_ccar", [128, NE * (KC - 1)], F32)
            m_y = AL("sbuf_tensor", "m_y", [128, 2 * 512], F32)
            m_ypr = AL("sbuf_tensor", "m_ypr", [128, NE * 512], BF16)
            m_wo = AL("sbuf_tensor", "m_wo", [128, 2 * 512], BF16)
            m_xr = AL("sbuf_tensor", "m_xr", [128, 512], F32)
            m_xo = AL("sbuf_tensor", "m_xo", [128, 512], F32)
            pA = AL("psum_tensor", "m_pA", [128, 512], F32)
            pB = AL("psum_tensor", "m_pB", [128, 512], F32)
            pC = AL("psum_tensor", "m_pC", [128, 512], F32)
            pD = AL("psum_tensor", "m_pD", [128, 512], F32)
            pE = AL("psum_tensor", "m_pE", [128, 512], F32)
            pF = AL("psum_tensor", "m_pF", [128, 512], F32)
            p96 = AL("psum_tensor", "m_p96", [96, 512], F32)

            SY(lambda s: s.dma_start(out=wdt_sb[:, :], in_=W_dt[:, :]))
            for ec in range(NE):
                SY(lambda s, ec=ec: s.dma_start(
                    out=m_xt[:, ec * 96:(ec + 1) * 96],
                    in_=W_x[ec * 128:(ec + 1) * 128, :]))
            wdt_ld = (sl.h, sl.n)
            wxc = VE(lambda v: v.tensor_copy(out=wx_bf[:, :],
                                             in_=m_xt[:, 0:NE * 96]),
                     waits=[wdt_ld])
            hcar_ms = GP(lambda g: g.memset(m_hcar[:, :], 0.0))
            ccar_ms = GP(lambda g: g.memset(m_ccar[:, :], 0.0))

            xn_marks = {}
            trm = {}
            xnTe_marks = {}
            winmm = {}
            stgevac = {}
            convlast = {}
            silu_marks = {}
            zevac = {}
            zst_marks = {}
            x96e = {}
            wxstop = {}
            bbe = {}
            cbe = {}
            relu_marks = {}
            deltaadd_marks = {}
            du_marks = {}
            xxq_marks = {}
            scan_marks = {}
            carry_marks = {}
            y_marks = {}
            ypr_marks = {}
            wdt_marks = {}
            dmm = {}
            dres = {}
            dstt = {}
            c1_marks = {}
            cmr_marks = {}
            y2_marks = {}
            bm3_marks = {}
            bq2_marks = {}

            def mg(d, k):
                v = d.get(k)
                return [v] if v is not None else []

            def tileA(c, jj):
                pb = 32 * (c % 2)
                cpar = c % 2
                ub = m_u[:, cpar * NE * 512:(cpar + 1) * NE * 512]
                x96 = m_x96[:, cpar * 512:(cpar + 1) * 512]
                j = 4 * c + jj
                xslot = m_xt[:, (jj % 2) * D:(jj % 2) * D + D]
                xm = SY(lambda s, j=j, xslot=xslot: s.dma_start(
                    out=xslot, in_=x0[j * 128:(j + 1) * 128, :]),
                    inc=r_mxl[jj % 2],
                    waits=mg(xn_marks, j - 2) + ([wxc] if j == 0 else []))
                cs = stat[:, pb + 8:pb + 9]
                cq = stat[:, pb + 9:pb + 10]
                cm_ = stat[:, pb + 10:pb + 11]
                cvv = stat[:, pb + 11:pb + 12]
                ctmp = stat[:, pb + 12:pb + 13]
                crq = stat[:, pb + 13:pb + 14]
                rs1 = stat[:, pb + 14:pb + 15]
                msq = stat[:, pb + 15:pb + 16]
                rs2 = stat[:, pb + 16:pb + 17]
                stok = stat[:, pb + 0 + jj:pb + 1 + jj]
                btok = stat[:, pb + 4 + jj:pb + 5 + jj]
                c1 = stat[:, pb + 24 + jj:pb + 25 + jj]
                cmr = stat[:, pb + 28 + jj:pb + 29 + jj]
                colwar = (mg(dstt, (c - 2, 1, 3)) + mg(dres, (c - 2, 1, 3)))
                s1 = SC(lambda s, xslot=xslot, cs=cs: s.activation(
                    out=m_xn[:, :], in_=xslot, func=AF.Identity,
                    accum_out=cs),
                    waits=[xm] + mg(trm, j - 1) + colwar)
                s2 = SC(lambda s, xslot=xslot, cq=cq: s.activation(
                    out=m_xn[:, :], in_=xslot, func=AF.Square,
                    accum_out=cq), waits=[s1])
                v1 = VE(lambda v, cs=cs, cm_=cm_: v.tensor_scalar(
                    out=cm_, in0=cs, scalar1=1.0 / D, scalar2=None,
                    op0=ALU.mult), waits=[s1] + colwar)
                v2 = VE(lambda v, cq=cq, cvv=cvv: v.tensor_scalar(
                    out=cvv, in0=cq, scalar1=1.0 / D, scalar2=None,
                    op0=ALU.mult), waits=[s2])
                v3 = VE(lambda v, cm_=cm_: v.tensor_tensor(
                    out=m_xn[:, 0:1], in0=cm_, in1=cm_, op=ALU.mult),
                    waits=[v1, s2])
                v4 = VE(lambda v, cvv=cvv: v.tensor_tensor(
                    out=cvv, in0=cvv, in1=m_xn[:, 0:1], op=ALU.subtract),
                    waits=[v2, v3])
                r1 = VE(lambda v, cvv=cvv, ctmp=ctmp: v.tensor_scalar(
                    out=ctmp, in0=cvv, scalar1=1e-5, scalar2=None,
                    op0=ALU.add), waits=[v4])
                r2 = VE(lambda v, ctmp=ctmp, crq=crq: v.reciprocal(
                    out=crq, in_=ctmp), waits=[r1])
                r3 = SC(lambda s, crq=crq, rs1=rs1: s.activation(
                    out=rs1, in_=crq, func=AF.Sqrt), waits=[r2])
                m1 = VE(lambda v, cvv=cvv, crq=crq, msq=msq: v.tensor_tensor(
                    out=msq, in0=cvv, in1=crq, op=ALU.mult), waits=[r2])
                m2 = VE(lambda v, msq=msq, ctmp=ctmp: v.tensor_scalar(
                    out=ctmp, in0=msq, scalar1=1e-6, scalar2=None,
                    op0=ALU.add), waits=[m1])
                m3 = VE(lambda v, ctmp=ctmp, crq=crq: v.reciprocal(
                    out=crq, in_=ctmp), waits=[m2, r3])
                m4 = SC(lambda s, crq=crq, rs2=rs2: s.activation(
                    out=rs2, in_=crq, func=AF.Sqrt), waits=[m3, r3])
                t1 = VE(lambda v, rs1=rs1, rs2=rs2, stok=stok: v.tensor_tensor(
                    out=stok, in0=rs1, in1=rs2, op=ALU.mult),
                    waits=[m4] + colwar)
                t2 = VE(lambda v, cm_=cm_, stok=stok, btok=btok:
                        v.tensor_tensor(out=btok, in0=cm_, in1=stok,
                                        op=ALU.mult), waits=[t1] + colwar)
                t3 = VE(lambda v, btok=btok: v.tensor_scalar(
                    out=btok, in0=btok, scalar1=-1.0, scalar2=None,
                    op0=ALU.mult), waits=[t2])
                c1_marks[(c, jj)] = VE(lambda v, rs1=rs1, c1=c1:
                                       v.tensor_scalar(
                    out=c1, in0=rs1, scalar1=1.0, scalar2=None, op0=ALU.add),
                    waits=[r3] + colwar)
                t5 = VE(lambda v, cm_=cm_, rs1=rs1, cmr=cmr: v.tensor_tensor(
                    out=cmr, in0=cm_, in1=rs1, op=ALU.mult),
                    waits=[r3] + colwar)
                cmr_marks[(c, jj)] = VE(lambda v, cmr=cmr: v.tensor_scalar(
                    out=cmr, in0=cmr, scalar1=-1.0, scalar2=None,
                    op0=ALU.mult), waits=[t5])
                xn_marks[j] = SC(lambda s, xslot=xslot, stok=stok, btok=btok:
                                 s.activation(out=m_xn[:, :], in_=xslot,
                                              func=AF.Identity, scale=stok,
                                              bias=btok),
                                 waits=[t3, v3, s2] + mg(trm, j - 1))
                for cc in range(8):
                    bank = pA if cc < 4 else pB
                    q = cc % 4
                    bankwar = mg(xnTe_marks, j - 1)
                    trm[j] = TE(lambda t, cc=cc, bank=bank, q=q: t.transpose(
                        out=bank[:, q * 128:(q + 1) * 128],
                        in_=m_xn[:, cc * 128:(cc + 1) * 128],
                        identity=ident_f[:, :]),
                        waits=[xn_marks[j]] + bankwar)
                for cc in range(8):
                    bank = pA if cc < 4 else pB
                    q = cc % 4
                    xnTe_marks[j] = SC(lambda s, cc=cc, bank=bank, q=q, jj=jj:
                                       s.activation(
                        out=m_xnT[:, cc * 512 + jj * 128:
                                  cc * 512 + (jj + 1) * 128],
                        in_=bank[:, q * 128:(q + 1) * 128], func=AF.Identity,
                        scale=norm_sb[:, cc:cc + 1]),
                        waits=[trm[j]]
                        + (mg(winmm, (c - 1) * 2 * NE + 2 * NE - 1)
                           if jj == 0 and cc == 0 else []))
            def etB(c, et):
                pb = 32 * (c % 2)
                cpar = c % 2
                ub = m_u[:, cpar * NE * 512:(cpar + 1) * NE * 512]
                x96 = m_x96[:, cpar * 512:(cpar + 1) * 512]
                ge = c * 2 * NE + et
                wslot = m_win[:, (et % 2) * 1024:(et % 2) * 1024 + 1024]
                Qw = SY if et % 2 == 0 else GP
                wm = Qw(lambda s, et=et, wslot=wslot: s.dma_start(
                    out=wslot[:, 0:1024],
                    in_=wbin_t[et, :, :]), inc=r_win[et % 2], k=16,
                    waits=mg(winmm, ge - 2))
                bank = pC if et % 2 == 0 else pD
                if et >= 2:
                    bwar = mg(stgevac, ge - 2) + mg(zevac, ge - 2)
                else:
                    bwar = mg(dres, (c - 2, 1, 2 + et))
                mmm = None
                for cc in range(8):
                    mmm = TE(lambda t, cc=cc, bank=bank, wslot=wslot: t.matmul(
                        bank[:, :], wslot[:, cc * 128:(cc + 1) * 128],
                        m_xnT[:, cc * 512:(cc + 1) * 512],
                        start=(cc == 0), stop=(cc == 7)),
                        inc=st if cc == 7 else None,
                        waits=([wm, xnTe_marks[4 * c + 3]] + bwar)
                        if cc == 0 else ())
                winmm[ge] = mmm
                if et < NE:
                    stg = m_stage[:, (et % 2) * 516:(et % 2) * 516 + 516]
                    sev = SC(lambda s, bank=bank, stg=stg: s.activation(
                        out=stg[:, 3:515], in_=bank[:, :], func=AF.Identity),
                        waits=[mmm] + mg(convlast, ge - 2))
                    stgevac[ge] = sev
                    cy = m_cv[:, (et % 2) * 512:(et % 2) * 512 + 512]
                    k0 = VE(lambda v, et=et, stg=stg: v.tensor_copy(
                        out=stg[:, 0:3], in_=m_ccar[:, et * 3:(et + 1) * 3]),
                        waits=[sev, (sg.h, ccar_ms[1])]
                        + mg(convlast, ge - 2))
                    k1 = VE(lambda v, et=et, stg=stg: v.tensor_copy(
                        out=m_ccar[:, et * 3:(et + 1) * 3],
                        in_=stg[:, 512:515]), waits=[k0])
                    k2 = VE(lambda v, et=et, stg=stg, cy=cy: v.tensor_scalar(
                        out=cy, in0=stg[:, 0:512],
                        scalar1=conv_sb[:, et * 4:et * 4 + 1], scalar2=None,
                        op0=ALU.mult),
                        waits=[k0] + mg(silu_marks, ge - 2))
                    for k in range(1, 4):
                        k2 = VE(lambda v, et=et, k=k, stg=stg, cy=cy:
                                v.scalar_tensor_tensor(
                            out=cy, in0=stg[:, k:512 + k],
                            scalar=conv_sb[:, et * 4 + k:et * 4 + k + 1],
                            in1=cy, op0=ALU.mult, op1=ALU.add), waits=[k2])
                    convlast[ge] = k2
                    silu_marks[ge] = SC(lambda s, et=et, cy=cy, ub=ub:
                                        s.activation(
                        out=ub[:, et * 512:(et + 1) * 512], in_=cy,
                        func=AF.Silu),
                        waits=[k2]
                        + (mg(wxstop, c - 2) + mg(du_marks, (c - 2) * NE + et)
                           + mg(y2_marks, (c - 2) * NE + et)
                           if c >= 2 else []))
                else:
                    ez = et - NE
                    gz = c * NE + ez
                    zslot = m_szs[:, (ez % 2) * 512:(ez % 2) * 512 + 512]
                    zev = SC(lambda s, bank=bank, zslot=zslot: s.activation(
                        out=zslot, in_=bank[:, :], func=AF.Silu),
                        waits=[mmm] + mg(zst_marks, gz - 2))
                    zevac[ge] = zev
                    zst_marks[gz] = SY(lambda s, ez=ez, c=c, zslot=zslot:
                                       s.dma_start(
                        out=z_hbm[ez * 128:(ez + 1) * 128,
                                  c * 512:(c + 1) * 512], in_=zslot),
                        inc=r_zst[ez % 2], k=16, waits=[zev])
            def wxAB(c):
                pb = 32 * (c % 2)
                cpar = c % 2
                ub = m_u[:, cpar * NE * 512:(cpar + 1) * NE * 512]
                x96 = m_x96[:, cpar * 512:(cpar + 1) * 512]
                wxm = None
                for ec in range(NE):
                    wxm = TE(lambda t, ec=ec, ub=ub: t.matmul(
                        p96[:, :], wx_bf[:, ec * 96:(ec + 1) * 96],
                        ub[:, ec * 512:(ec + 1) * 512],
                        start=(ec == 0), stop=(ec == NE - 1)),
                        inc=st if ec == NE - 1 else None,
                        waits=[silu_marks[c * 2 * NE + ec], wxc]
                        + (mg(x96e, c - 1) if ec == 0 else []))
                wxstop[c] = wxm
                x96e[c] = VE(lambda v, x96=x96: v.tensor_copy(
                    out=x96[:, :], in_=p96[0:96, :]),
                    waits=[wxm] + (mg(wdt_marks, (c - 2) * NE + NE - 1)
                                   if c >= 2 else [])
                    + ([(r_zbc[0].h, r_zbc[0].n)] if c >= 2 else []))
            def bcastC(c, n):
                pb = 32 * (c % 2)
                cpar = c % 2
                ub = m_u[:, cpar * NE * 512:(cpar + 1) * NE * 512]
                x96 = m_x96[:, cpar * 512:(cpar + 1) * 512]
                par = n % 2
                rsl = m_bcr[0:1, 0:1024]
                bsl = m_bcb[0:1, 0:1024]
                bm1 = GP(lambda g, n=n, rsl=rsl, x96=x96: g.dma_start(
                    out=rsl[0:1, 0:512], in_=x96[64 + n:65 + n, :]),
                    inc=r_zbc[0], k=16,
                    waits=[x96e[c]] + (mg(bm3_marks, (c, n - 1)) if n >= 1
                                       else mg(bm3_marks,
                                               (c - 1, NS - 1))))
                bm2 = GP(lambda g, n=n, rsl=rsl, x96=x96: g.dma_start(
                    out=rsl[0:1, 512:1024], in_=x96[80 + n:81 + n, :]),
                    inc=r_zbc[0], k=16)
                bm3 = VE(lambda v, rsl=rsl, bsl=bsl: v.tensor_copy(
                    out=bsl, in_=rsl),
                    waits=[(r_zbc[0].h, r_zbc[0].n)]
                    + (mg(bq2_marks, (c, n - 1)) if n >= 1 else
                       mg(bq2_marks, (c - 1, NS - 1))))
                bm3_marks[(c, n)] = bm3
                ebank = pE if par == 0 else pF
                if n >= 2:
                    ewar = mg(bbe, (c, n - 2)) + mg(cbe, (c, n - 2))
                else:
                    ewar = (mg(relu_marks, (c - 1) * NE + NE - 2 + n)
                            + mg(dres, (c - 1, 1, n)))
                bq1 = TE(lambda t, bsl=bsl, ebank=ebank: t.matmul(
                    ebank[:, :], ones_b[:, :], bsl[0:1, 0:512],
                    start=True, stop=True), waits=[bm3] + ewar)
                bbe[(c, n)] = SC(lambda s, n=n, ebank=ebank: s.activation(
                    out=m_Bb[:, n * 512:(n + 1) * 512], in_=ebank[:, :],
                    func=AF.Identity),
                    waits=[bq1]
                    + (mg(xxq_marks, (c - 1) * NE * 4 + NE * 4 - 1)
                       if n == 0 else []))
                bq2 = TE(lambda t, bsl=bsl, ebank=ebank: t.matmul(
                    ebank[:, :], ones_b[:, :], bsl[0:1, 512:1024],
                    start=True, stop=True), waits=[bbe[(c, n)]])
                bq2_marks[(c, n)] = bq2
                cbe[(c, n)] = VE(lambda v, n=n, ebank=ebank: v.tensor_copy(
                    out=bass.AP(m_Cb, n, [[NS * 512, 128], [NS, 512]]),
                    in_=ebank[:, :]),
                    waits=[bq2]
                    + (mg(y_marks, (c - 1) * NE + NE - 1)
                       if n == 0 else []))
            def etCpre(c, et):
                pb = 32 * (c % 2)
                cpar = c % 2
                ub = m_u[:, cpar * NE * 512:(cpar + 1) * NE * 512]
                x96 = m_x96[:, cpar * 512:(cpar + 1) * 512]
                te = c * NE + et
                par = et % 2
                dslot = m_delta[:, par * 512:(par + 1) * 512]
                spslot = m_sp[:, par * 512:(par + 1) * 512]
                duslot = m_du[:, par * 512:(par + 1) * 512]
                zslot = m_szl[:, (et % 4) * 512:(et % 4 + 1) * 512]
                bank = pE if par == 0 else pF
                zrm = SY(lambda s, et=et, c=c, zslot=zslot: s.dma_start(
                    out=zslot, in_=z_hbm[et * 128:(et + 1) * 128,
                                         c * 512:(c + 1) * 512]),
                    inc=r_zrl[et % 4],
                    waits=[zst_marks[te]] + mg(ypr_marks, te - 4))
                wdt_marks[te] = TE(lambda t, et=et, bank=bank, x96=x96:
                                   t.matmul(
                    bank[:, :], wdt_sb[:, et * 128:(et + 1) * 128],
                    x96[0:64, :], start=True, stop=True),
                    waits=[x96e[c], wdt_ld, bbe[(c, NS - 2 + par)],
                           cbe[(c, NS - 2 + par)]]
                    + mg(relu_marks, te - 2))
                a1 = SC(lambda s, et=et, bank=bank, spslot=spslot:
                        s.activation(
                    out=spslot, in_=bank[:, :], func=AF.Abs,
                    bias=bdt_sb[:, et:et + 1]),
                    waits=[wdt_marks[te]] + mg(deltaadd_marks, te - 2))
                a2 = SC(lambda s, spslot=spslot: s.activation(
                    out=spslot, in_=spslot, func=AF.Exp, scale=-1.0),
                    waits=[a1])
                a3 = SC(lambda s, spslot=spslot: s.activation(
                    out=spslot, in_=spslot, func=AF.Ln, bias=cbias[:, 2:3]),
                    waits=[a2])
                relu_marks[te] = SC(lambda s, et=et, bank=bank, dslot=dslot:
                                    s.activation(
                    out=dslot, in_=bank[:, :], func=AF.Relu,
                    bias=bdt_sb[:, et:et + 1]),
                    waits=[a3] + mg(du_marks, te - 2))
                deltaadd_marks[te] = VE(lambda v, dslot=dslot, spslot=spslot:
                                        v.tensor_tensor(
                    out=dslot, in0=dslot, in1=spslot, op=ALU.add),
                    waits=[relu_marks[te]])
                du_marks[te] = VE(lambda v, et=et, dslot=dslot, ub=ub,
                                  duslot=duslot: v.tensor_tensor(
                    out=duslot, in0=dslot,
                    in1=ub[:, et * 512:(et + 1) * 512], op=ALU.mult),
                    waits=[deltaadd_marks[te], silu_marks[c * 2 * NE + et]]
                    + (mg(xxq_marks, (te - 2) * 4 + 3)))

            def etCscan(c, et):
                pb = 32 * (c % 2)
                cpar = c % 2
                ub = m_u[:, cpar * NE * 512:(cpar + 1) * NE * 512]
                x96 = m_x96[:, cpar * 512:(cpar + 1) * 512]
                te = c * NE + et
                par = et % 2
                dslot = m_delta[:, par * 512:(par + 1) * 512]
                spslot = m_sp[:, par * 512:(par + 1) * 512]
                duslot = m_du[:, par * 512:(par + 1) * 512]
                zslot = m_szl[:, (et % 4) * 512:(et % 4 + 1) * 512]
                bank = pE if par == 0 else pF
                for q in range(4):
                    gq = te * 4 + q
                    xpar = gq % 2
                    xslot = m_xx[:, xpar * 2048:(xpar + 1) * 2048]
                    if gq >= 2:
                        pg = gq - 2
                        war = mg(scan_marks,
                                 (pg // 4) * 16 + (pg % 4) * 4 + 3)
                    else:
                        war = []
                    xxq_marks[gq] = GP(lambda g, q=q, et=et, xslot=xslot:
                                       g.tensor_tensor(
                        out=xslot,
                        in0=bass.AP(m_du, (et % 2) * 512,
                                    [[1024, 128], [0, 4], [1, 512]]),
                        in1=m_Bb[:, q * 4 * 512:(q + 1) * 4 * 512],
                        op=ALU.mult),
                        waits=[du_marks[te], bbe[(c, q * 4 + 3)]] + war)
                    for n in range(q * 4, q * 4 + 4):
                        gn = te * 16 + n
                        npar = n % 4
                        daslot = m_dA[:, npar * 512:(npar + 1) * 512]
                        em = SC(lambda s, et=et, n=n, dslot=dslot,
                                daslot=daslot: s.activation(
                            out=daslot, in_=dslot, func=AF.Exp,
                            scale=A_sb[:, et * NS + n:et * NS + n + 1]),
                            waits=[deltaadd_marks[te], relu_marks[te]]
                            + mg(scan_marks, gn - 4))
                        xs_ap = m_xx[:, xpar * 2048 + (n % 4) * 512:
                                     xpar * 2048 + (n % 4 + 1) * 512]
                        scan_marks[gn] = VE(
                            lambda v, et=et, n=n, daslot=daslot, xs_ap=xs_ap:
                            v.tensor_tensor_scan(
                                out=bass.AP(m_h, n,
                                            [[NS * 512, 128], [NS, 512]]),
                                data0=daslot, data1=xs_ap,
                                initial=m_hcar[:, et * NS + n:
                                               et * NS + n + 1],
                                op0=ALU.mult, op1=ALU.add),
                            waits=[em, xxq_marks[gq],
                                   (sg.h, hcar_ms[1])]
                            + (mg(y_marks, te - 1) if n == 0 else [])
                            + (mg(carry_marks, te - NE) if n == 0 else []))
                carry_marks[te] = VE(lambda v, et=et: v.tensor_copy(
                    out=m_hcar[:, et * NS:(et + 1) * NS],
                    in_=m_h[:, 511 * NS:512 * NS]),
                    waits=[scan_marks[te * 16 + NS - 1]])
                ym1 = VE(lambda v: v.tensor_tensor(
                    out=m_h[:, :], in0=m_h[:, :], in1=m_Cb[:, :],
                    op=ALU.mult),
                    waits=[carry_marks[te], cbe[(c, NS - 1)]])
                ym2 = VE(lambda v: v.tensor_reduce(
                    out=m_y[:, 0:512],
                    in_=bass.AP(m_h, 0, [[NS * 512, 128], [NS, 512], [1, NS]]),
                    axis=AX.X, op=ALU.add),
                    waits=[ym1] + (mg(dstt, (c - 1, 1, 3))
                                   if et == 0 else []))
                y_marks[te] = ym2
                ym3 = VE(lambda v, et=et, ub=ub: v.scalar_tensor_tensor(
                    out=m_y[:, 512:1024],
                    in0=ub[:, et * 512:(et + 1) * 512],
                    scalar=D_sb[:, et:et + 1], in1=m_y[:, 0:512],
                    op0=ALU.mult, op1=ALU.add),
                    waits=[ym2, silu_marks[c * 2 * NE + et]])
                y2_marks[te] = ym3
                ypr_marks[te] = VE(lambda v, et=et, zslot=zslot:
                                   v.tensor_tensor(
                    out=m_ypr[:, et * 512:(et + 1) * 512],
                    in0=m_y[:, 512:1024], in1=zslot, op=ALU.mult),
                    waits=[ym3, (r_zrl[et % 4].h, r_zrl[et % 4].n)]
                    + (mg(dmm, (c - 1, 1, NE - 1, 3)) if et < 2 else []))
            def stageD(c):
                pb = 32 * (c % 2)
                cpar = c % 2
                ub = m_u[:, cpar * NE * 512:(cpar + 1) * NE * 512]
                x96 = m_x96[:, cpar * 512:(cpar + 1) * 512]
                for half in range(2):
                    for et in range(NE):
                        wslot = m_wo[:, (et % 2) * 512:(et % 2) * 512 + 512]
                        wom = SY(lambda s, half=half, et=et, wslot=wslot:
                                 s.dma_start(
                            out=wslot, in_=wbout_t[half, et, :, :]),
                            inc=r_wou[et % 2],
                            waits=mg(dmm, (c, half, et - 2, 3))
                            + mg(dmm, (c - 1, 1, NE + et - 2 if False else
                                       NE - 2 + et, 3)))
                        for jj in range(4):
                            bank = [pE, pF, pC, pD][jj]
                            if et == 0:
                                if half == 0:
                                    if jj < 2:
                                        bwar = [relu_marks[c * NE + NE - 2 + jj]]
                                    else:
                                        bwar = (mg(zevac, (c + 1) * 2 * NE
                                                   + 2 * NE - 4 + jj)
                                                + mg(zevac, c * 2 * NE
                                                     + 2 * NE - 4 + jj))
                                else:
                                    bwar = mg(dres, (c, 0, jj))
                            else:
                                bwar = []
                            dmm[(c, half, et, jj)] = TE(
                                lambda t, et=et, jj=jj, bank=bank, wslot=wslot:
                                t.matmul(
                                    bank[:, :],
                                    m_ypr[:, et * 512 + jj * 128:
                                          et * 512 + (jj + 1) * 128],
                                    wslot, start=(et == 0), stop=(et == NE - 1)),
                                inc=st if jj == 3 else None,
                                waits=([wom, ypr_marks[c * NE + et]] + bwar)
                                if (jj == 0 or et == 0) else ())
                    for jj in range(4):
                        j = 4 * c + jj
                        bank = [pE, pF, pC, pD][jj]
                        xrm = SY(lambda s, j=j, half=half: s.dma_start(
                            out=m_xr[:, :], in_=x0[j * 128:(j + 1) * 128,
                                                   half * 512:(half + 1) * 512]),
                            inc=r_mxr[0],
                            waits=mg(dstt, (c, half, jj - 1))
                            + mg(dstt, (c, half - 1, 3))
                            + mg(dstt, (c - 1, 1, 3)))
                        dres[(c, half, jj)] = SC(
                            lambda s, jj=jj, bank=bank, pb=pb: s.activation(
                                out=m_y[:, (jj % 2) * 512:(jj % 2) * 512 + 512],
                                in_=bank[:, :], func=AF.Identity,
                                bias=stat[:, pb + 28 + jj:pb + 29 + jj]),
                            waits=[dmm[(c, half, NE - 1, 3)],
                                   ypr_marks[c * NE + NE - 1],
                                   cmr_marks[(c, jj)]]
                            + mg(dstt, (c, half, jj - 2))
                            + (mg(dstt, (c, half - 1, 2 + jj)) if jj < 2 else []))
                        dstt[(c, half, jj)] = VE(
                            lambda v, jj=jj, pb=pb: v.scalar_tensor_tensor(
                                out=m_xo[:, :], in0=m_xr[:, :],
                                scalar=stat[:, pb + 24 + jj:pb + 25 + jj],
                                in1=m_y[:, (jj % 2) * 512:(jj % 2) * 512 + 512],
                                op0=ALU.mult, op1=ALU.add),
                            waits=[xrm, dres[(c, half, jj)],
                                   c1_marks[(c, jj)], (s_dst.h, s_dst.n)])
                        SY(lambda s, j=j, half=half: s.dma_start(
                            out=xs1[j * 128:(j + 1) * 128,
                                    half * 512:(half + 1) * 512], in_=m_xo[:, :]),
                            inc=s_dst, k=16, waits=[dstt[(c, half, jj)]])

            for jj in range(4):
                tileA(0, jj)
            for et in range(2 * NE):
                etB(0, et)
            wxAB(0)
            for c in range(NCH):
                for n in range(NS):
                    bcastC(c, n)
                etCpre(c, 0)
                etCpre(c, 1)
                etCscan(c, 0)
                if c + 1 < NCH:
                    for jj in range(4):
                        tileA(c + 1, jj)
                for et in range(1, 7):
                    etCpre(c, et + 1)
                    etCscan(c, et)
                for et in range(7, NE - 1):
                    if c + 1 < NCH:
                        for eb in range(4 * (et - 7), 4 * (et - 7) + 4):
                            etB(c + 1, eb)
                    etCpre(c, et + 1)
                    etCscan(c, et)
                if c + 1 < NCH:
                    wxAB(c + 1)
                etCscan(c, NE - 1)
                stageD(c)
            pm.run()

    def phase_ffn(xin_ap, xout_ap, tag):
        pf = new_prog()
        with ExitStack() as esp:
            AL = lambda k, n, s, d: esp.enter_context(getattr(nc, k)(n, list(s), d))
            f_wg = AL("sbuf_tensor", tag + "_wg", [128, 8 * INNER], F8)
            f_wu = AL("sbuf_tensor", tag + "_wu", [128, 8 * INNER], F8)
            f_wd = AL("sbuf_tensor", tag + "_wd", [128, 64 * 512], F8)
            f_xt = AL("sbuf_tensor", tag + "_xt", [128, 2 * D], F32)
            f_xln = AL("sbuf_tensor", tag + "_xln", [128, D], F32)
            f_xqi = AL("sbuf_tensor", tag + "_xqi", [128, D], F32)
            f_xqT = AL("sbuf_tensor", tag + "_xqT", [128, 2 * D], BF16)
            f_g = AL("sbuf_tensor", tag + "_g", [128, 2 * INNER], F32)
            f_hqi = AL("sbuf_tensor", tag + "_hqi", [128, INNER], F32)
            f_hqT = AL("sbuf_tensor", tag + "_hqT", [128, INNER], BF16)
            f_xo = AL("sbuf_tensor", tag + "_xo", [128, D], F32)
            fpA = AL("psum_tensor", tag + "_pA", [128, 512], F32)
            fpB = AL("psum_tensor", tag + "_pB", [128, 512], F32)
            fpC = AL("psum_tensor", tag + "_pC", [128, 512], F32)
            fpD = AL("psum_tensor", tag + "_pD", [128, 512], F32)
            fpX = AL("psum_tensor", tag + "_pX", [128, 512], F32)
            fpY = AL("psum_tensor", tag + "_pY", [128, 512], F32)

            for cc in range(8):
                for ob in range(8):
                    SY(lambda s, cc=cc, ob=ob: s.dma_start(
                        out=f_wg[:, (cc * 8 + ob) * 512:(cc * 8 + ob + 1) * 512],
                        in_=wg8[cc, ob, :, :]))
                    SY(lambda s, cc=cc, ob=ob: s.dma_start(
                        out=f_wu[:, (cc * 8 + ob) * 512:(cc * 8 + ob + 1) * 512],
                        in_=wu8[cc, ob, :, :]))
            w_ready = (sl.h, sl.n)
            for half in range(2):
                for oc in range(32):
                    SD(lambda s, oc=oc, half=half: s.dma_start(
                        out=f_wd[:, (half * 32 + oc) * 512:
                                 (half * 32 + oc + 1) * 512],
                        in_=wd8[oc, half, :, :]))
            wd_ready = (sl2.h, sl2.n)

            xm_m = {}
            xln_m = {}
            amax_m = {}
            sx_m = {}
            ivg_m = {}
            ivu_m = {}
            ivd_m = {}
            xqi_m = {}
            xqtr_m = {}
            xqte_m = {}
            gmm = {}
            umm = {}
            gev = {}
            hstt = {}
            amaxh_m = {}
            hqi_m = {}
            hqtr_m = {}
            hqte_m = {}
            dmmf = {}
            dstt_m = {}
            st_m = {}

            def mg(d, k):
                v = d.get(k)
                return [v] if v is not None else []

            def front(j):
                pb = 24 * (j % 2)
                xslot = f_xt[:, (j % 2) * D:(j % 2) * D + D]
                xqTs = f_xqT[:, (j % 2) * D:(j % 2) * D + D]
                xm = SY(lambda s, j=j, xslot=xslot: s.dma_start(
                    out=xslot, in_=xin_ap[j * 128:(j + 1) * 128, :]),
                    inc=r_fxl[j % 2], waits=mg(dstt_m, (j - 2, 1)))
                xm_m[j] = xm
                cs = stat[:, pb + 0:pb + 1]
                cq = stat[:, pb + 1:pb + 2]
                cm_ = stat[:, pb + 2:pb + 3]
                cvv = stat[:, pb + 3:pb + 4]
                ctmp = stat[:, pb + 4:pb + 5]
                crq = stat[:, pb + 5:pb + 6]
                rs_ = stat[:, pb + 6:pb + 7]
                nb_ = stat[:, pb + 7:pb + 8]
                amx = stat[:, pb + 8:pb + 9]
                sx = stat[:, pb + 9:pb + 10]
                ivg = stat[:, pb + 10:pb + 11]
                ivu = stat[:, pb + 11:pb + 12]
                cmx = stat[:, pb + 15:pb + 16]
                cmn = stat[:, pb + 16:pb + 17]
                qsc = stat[:, pb + 17:pb + 18]
                qbi = stat[:, pb + 18:pb + 19]
                colwar = mg(dstt_m, (j - 2, 1)) + mg(ivd_m, j - 2)
                s1 = SC(lambda s, xslot=xslot, cs=cs: s.activation(
                    out=f_xln[:, :], in_=xslot, func=AF.Identity, accum_out=cs),
                    waits=[xm] + mg(amax_m, j - 1) + mg(xqi_m, j - 1) + colwar)
                s2 = SC(lambda s, xslot=xslot, cq=cq: s.activation(
                    out=f_xln[:, :], in_=xslot, func=AF.Square, accum_out=cq),
                    waits=[s1])
                w1 = VE(lambda v, xslot=xslot, cmx=cmx: v.tensor_reduce(
                    out=cmx, in_=xslot, axis=AX.X, op=ALU.max),
                    waits=[xm] + colwar)
                w2 = VE(lambda v, xslot=xslot, cmn=cmn: v.tensor_reduce(
                    out=cmn, in_=xslot, axis=AX.X, op=ALU.min), waits=[w1])
                v1 = VE(lambda v, cs=cs, cm_=cm_: v.tensor_scalar(
                    out=cm_, in0=cs, scalar1=1.0 / D, scalar2=None,
                    op0=ALU.mult), waits=[s1] + colwar)
                v2 = VE(lambda v, cq=cq, cvv=cvv: v.tensor_scalar(
                    out=cvv, in0=cq, scalar1=1.0 / D, scalar2=None,
                    op0=ALU.mult), waits=[s2])
                v3 = VE(lambda v, cm_=cm_: v.tensor_tensor(
                    out=f_xln[:, 0:1], in0=cm_, in1=cm_, op=ALU.mult),
                    waits=[v1, s2])
                v4 = VE(lambda v, cvv=cvv: v.tensor_tensor(
                    out=cvv, in0=cvv, in1=f_xln[:, 0:1], op=ALU.subtract),
                    waits=[v2, v3])
                r1 = VE(lambda v, cvv=cvv, ctmp=ctmp: v.tensor_scalar(
                    out=ctmp, in0=cvv, scalar1=1e-5, scalar2=None, op0=ALU.add),
                    waits=[v4])
                r2 = VE(lambda v, ctmp=ctmp, crq=crq: v.reciprocal(
                    out=crq, in_=ctmp), waits=[r1])
                r3 = SC(lambda s, crq=crq, rs_=rs_: s.activation(
                    out=rs_, in_=crq, func=AF.Sqrt), waits=[r2])
                g1 = VE(lambda v, cmx=cmx, cm_=cm_: v.tensor_tensor(
                    out=cmx, in0=cmx, in1=cm_, op=ALU.subtract),
                    waits=[w2, v1])
                g2 = VE(lambda v, cm_=cm_, cmn=cmn: v.tensor_tensor(
                    out=cmn, in0=cm_, in1=cmn, op=ALU.subtract), waits=[g1])
                g3 = VE(lambda v, cmx=cmx, cmn=cmn: v.tensor_tensor(
                    out=cmx, in0=cmx, in1=cmn, op=ALU.max), waits=[g2])
                g4 = VE(lambda v, cmx=cmx, rs_=rs_, amx=amx: v.tensor_tensor(
                    out=amx, in0=cmx, in1=rs_, op=ALU.mult),
                    waits=[g3, r3] + colwar)
                a2 = VE(lambda v, amx=amx: v.tensor_scalar(
                    out=amx, in0=amx, scalar1=1e-5, scalar2=1.0 / 127.0,
                    op0=ALU.max, op1=ALU.mult), waits=[g4])
                sx_m[j] = VE(lambda v, amx=amx, sx=sx: v.reciprocal(
                    out=sx, in_=amx), waits=[a2])
                ivg_m[j] = VE(lambda v, amx=amx, ivg=ivg: v.tensor_tensor(
                    out=ivg, in0=amx, in1=rswg[:, 0:1], op=ALU.mult),
                    waits=[sx_m[j]])
                ivu_m[j] = VE(lambda v, amx=amx, ivu=ivu: v.tensor_tensor(
                    out=ivu, in0=amx, in1=rswg[:, 1:2], op=ALU.mult),
                    waits=[ivg_m[j]])
                q1 = VE(lambda v, rs_=rs_, sx=sx, qsc=qsc: v.tensor_tensor(
                    out=qsc, in0=rs_, in1=sx, op=ALU.mult),
                    waits=[sx_m[j]] + colwar)
                q2 = VE(lambda v, cm_=cm_, qsc=qsc, qbi=qbi: v.tensor_tensor(
                    out=qbi, in0=cm_, in1=qsc, op=ALU.mult), waits=[q1])
                q3 = VE(lambda v, qbi=qbi: v.tensor_scalar(
                    out=qbi, in0=qbi, scalar1=-1.0, scalar2=12582912.0,
                    op0=ALU.mult, op1=ALU.add), waits=[q2])
                amax_m[j] = q3
                xln_m[j] = q3
                xqi_m[j] = SC(lambda s, xslot=xslot, qsc=qsc, qbi=qbi:
                              s.activation(
                    out=f_xqi[:, :], in_=xslot, func=AF.Identity, scale=qsc,
                    bias=qbi), waits=[q3, v3] + mg(xqtr_m, j - 1))
                for rnd in range(2):
                    bank = fpA if rnd == 0 else fpB
                    for cc4 in range(4):
                        cc = rnd * 4 + cc4
                        xqtr_m[j] = TE(lambda t, cc=cc, cc4=cc4, bank=bank:
                                       t.transpose(
                            out=bank[:, cc4 * 128:(cc4 + 1) * 128],
                            in_=f_xqi[:, cc * 128:(cc + 1) * 128],
                            identity=ident_f[:, :]),
                            waits=([xqi_m[j]]
                                   + mg(gev, (j - 1, 6 + rnd)))
                            if cc4 == 0 else [xqi_m[j]])
                    xqte_m[(j, rnd)] = SC(lambda s, rnd=rnd, xqTs=xqTs:
                                          s.activation(
                        out=xqTs[:, rnd * 512:(rnd + 1) * 512],
                        in_=(fpA if rnd == 0 else fpB)[:, 0:512],
                        func=AF.Identity, bias=cbias[:, 1:2]),
                        waits=[xqtr_m[j]]
                        + (mg(gmm, (j - 1, 6 + rnd))
                           + mg(umm, (j - 1, 6 + rnd))))

            def mid(j):
                pb = 24 * (j % 2)
                xqTs = f_xqT[:, (j % 2) * D:(j % 2) * D + D]
                gs = f_g[:, (j % 2) * INNER:(j % 2) * INNER + INNER]
                ivg = stat[:, pb + 10:pb + 11]
                ivu = stat[:, pb + 11:pb + 12]
                amh = stat[:, pb + 12:pb + 13]
                sh = stat[:, pb + 13:pb + 14]
                ivd = stat[:, pb + 14:pb + 15]
                for ob in range(8):
                    bg = fpA if ob % 2 == 0 else fpB
                    bu = fpC if ob % 2 == 0 else fpD
                    gwar = (mg(gev, (j, ob - 2)) if ob >= 2
                            else [xqte_m[(j, ob)]])
                    uwar = (mg(hstt, (j, ob - 2)) if ob >= 2
                            else mg(hqtr_m, (j - 1, 2 + 2 * ob)))
                    g0 = None
                    u0 = None
                    for cc in range(8):
                        g0 = TE(lambda t, cc=cc, ob=ob, bg=bg, xqTs=xqTs:
                                t.matmul(
                            bg[:, :], xqTs[:, cc * 128:(cc + 1) * 128],
                            f_wg[:, (cc * 8 + ob) * 512:(cc * 8 + ob + 1) * 512],
                            start=(cc == 0), stop=(cc == 7)),
                            inc=st if cc == 7 else None,
                            waits=([xqte_m[(j, 1)], w_ready] + gwar)
                            if cc == 0 else ())
                        u0 = TE(lambda t, cc=cc, ob=ob, bu=bu, xqTs=xqTs:
                                t.matmul(
                            bu[:, :], xqTs[:, cc * 128:(cc + 1) * 128],
                            f_wu[:, (cc * 8 + ob) * 512:(cc * 8 + ob + 1) * 512],
                            start=(cc == 0), stop=(cc == 7)),
                            inc=st if cc == 7 else None,
                            waits=(uwar if cc == 0 else ()))
                    gmm[(j, ob)] = g0
                    umm[(j, ob)] = u0
                    gev[(j, ob)] = SC(lambda s, ob=ob, bg=bg, ivg=ivg, gs=gs:
                                      s.activation(
                        out=gs[:, ob * 512:(ob + 1) * 512], in_=bg[:, :],
                        func=AF.Silu, scale=ivg),
                        waits=[g0, ivg_m[j]] + mg(amaxh_m, j - 2)
                        + mg(hqi_m, j - 2))
                    hstt[(j, ob)] = VE(lambda v, ob=ob, bu=bu, ivu=ivu, gs=gs:
                                       v.scalar_tensor_tensor(
                        out=gs[:, ob * 512:(ob + 1) * 512], in0=bu[:, :],
                        scalar=ivu, in1=gs[:, ob * 512:(ob + 1) * 512],
                        op0=ALU.mult, op1=ALU.mult),
                        waits=[u0, gev[(j, ob)], ivu_m[j]])
                colwar = mg(dstt_m, (j - 2, 1)) + mg(ivd_m, j - 2)
                amaxh_m[j] = VE(lambda v, amh=amh, gs=gs: v.tensor_reduce(
                    out=amh, in_=gs[:, :], axis=AX.X, op=ALU.max,
                    apply_absolute_value=True), waits=[hstt[(j, 7)]] + colwar)
                h2 = VE(lambda v, amh=amh: v.tensor_scalar(
                    out=amh, in0=amh, scalar1=1e-5, scalar2=1.0 / 127.0,
                    op0=ALU.max, op1=ALU.mult), waits=[amaxh_m[j]])
                h3 = VE(lambda v, amh=amh, sh=sh: v.reciprocal(
                    out=sh, in_=amh), waits=[h2])
                ivd_m[j] = VE(lambda v, amh=amh, ivd=ivd: v.tensor_tensor(
                    out=ivd, in0=amh, in1=rswg[:, 2:3], op=ALU.mult),
                    waits=[h3])
                hqi_m[j] = SC(lambda s, sh=sh, gs=gs: s.activation(
                    out=f_hqi[:, :], in_=gs[:, :], func=AF.Identity, scale=sh,
                    bias=cbias[:, 0:1]),
                    waits=[h3, hstt[(j, 7)]] + mg(hqtr_m, (j - 1, 7)))

            def tail(j):
                pb = 24 * (j % 2)
                xslot = f_xt[:, (j % 2) * D:(j % 2) * D + D]
                ivd = stat[:, pb + 14:pb + 15]
                for grp in range(8):
                    bank = fpC if grp % 2 == 0 else fpD
                    for q in range(4):
                        oc = grp * 4 + q
                        hqtr_m[(j, grp)] = TE(lambda t, oc=oc, q=q, bank=bank:
                                              t.transpose(
                            out=bank[:, q * 128:(q + 1) * 128],
                            in_=f_hqi[:, oc * 128:(oc + 1) * 128],
                            identity=ident_f[:, :]),
                            waits=([hqi_m[j]] + mg(hqte_m, (j, grp - 2))
                                   + ([umm[(j, 6 + grp)]] if grp < 2 else []))
                            if q == 0 else [hqi_m[j]])
                    hqte_m[(j, grp)] = SC(lambda s, grp=grp, bank=bank:
                                          s.activation(
                        out=f_hqT[:, grp * 512:(grp + 1) * 512],
                        in_=bank[:, 0:512], func=AF.Identity,
                        bias=cbias[:, 1:2]),
                        waits=[hqtr_m[(j, grp)]] + mg(dmmf, (j - 1, 1, 3)))
                for half in range(2):
                    bank = fpX if half == 0 else fpY
                    for grp in range(4):
                        for q in range(8):
                            oc = grp * 8 + q
                            wslot = f_wd[:, (half * 32 + oc) * 512:
                                         (half * 32 + oc + 1) * 512]
                            dmmf[(j, half, grp)] = TE(
                                lambda t, oc=oc, bank=bank, wslot=wslot:
                                t.matmul(
                                    bank[:, :],
                                    f_hqT[:, oc * 128:(oc + 1) * 128], wslot,
                                    start=(oc == 0), stop=(oc == 31)),
                                inc=st if oc % 8 == 7 else None,
                                waits=([wd_ready, hqte_m[(j, grp * 2 + 1)]]
                                       + (mg(dstt_m, (j - 1, half))
                                          if grp == 0 else []))
                                if q == 0 else ())
                    dstt_m[(j, half)] = VE(
                        lambda v, half=half, bank=bank, xslot=xslot, ivd=ivd:
                        v.scalar_tensor_tensor(
                            out=f_xo[:, half * 512:(half + 1) * 512],
                            in0=bank[:, :], scalar=ivd,
                            in1=xslot[:, half * 512:(half + 1) * 512],
                            op0=ALU.mult, op1=ALU.add),
                        waits=[dmmf[(j, half, 3)], ivd_m[j], xm_m[j],
                               (sw.h, sw.n)])
                st_m[j] = GP(lambda g, j=j: g.dma_start(
                    out=xout_ap[j * 128:(j + 1) * 128, :], in_=f_xo[:, :]),
                    inc=sw, k=16, waits=[dstt_m[(j, 0)], dstt_m[(j, 1)]])

            for j in range(NT):
                front(j)
                if j >= 1:
                    tail(j - 1)
                mid(j)
            tail(NT - 1)
        pf.run()

    if "F1" in phases:
        phase_ffn(xs1, xs2, "f1")

    if "A" in phases:
        pa = new_prog()
        scale2 = 1.0 / (float(T) * float(T))
        with ExitStack() as esp:
            AL = lambda k, n, s, d: esp.enter_context(getattr(nc, k)(n, list(s), d))
            a_w = AL("sbuf_tensor", "a_w", [128, 5 * 2 * 8 * 512], BF16)
            a_lngb = AL("sbuf_tensor", "a_lngb", [128, D], F32)
            a_lngr = AL("sbuf_tensor", "a_lngr", [1, D], F32)
            a_xt = AL("sbuf_tensor", "a_xt", [128, 2 * D], F32)
            a_xln = AL("sbuf_tensor", "a_xln", [128, D], F32)
            a_xlnT = AL("sbuf_tensor", "a_xlnT", [128, D], BF16)
            a_f = AL("sbuf_tensor", "a_f", [128, 2 * D], F32)
            a_sq = AL("sbuf_tensor", "a_sq", [128, D], F32)
            a_nrm = AL("sbuf_tensor", "a_nrm", [128, 2 * H], F32)
            a_qn = AL("sbuf_tensor", "a_qn", [128, D], F32)
            a_kn = AL("sbuf_tensor", "a_kn", [128, D], BF16)
            a_vn = AL("sbuf_tensor", "a_vn", [128, D], BF16)
            a_st = AL("sbuf_tensor", "a_st", [128, 2 * 1024], F32)
            a_S = AL("sbuf_tensor", "a_S", [128, 8 * 64], BF16)
            a_qT = AL("sbuf_tensor", "a_qT", [128, D], BF16)
            a_og = AL("sbuf_tensor", "a_og", [128, D], F32)
            a_ogT = AL("sbuf_tensor", "a_ogT", [128, D], BF16)
            a_xo = AL("sbuf_tensor", "a_xo", [128, D], F32)
            apA = AL("psum_tensor", "a_pA", [128, 512], F32)
            apB = AL("psum_tensor", "a_pB", [128, 512], F32)
            apC = AL("psum_tensor", "a_pC", [128, 512], F32)
            apD = AL("psum_tensor", "a_pD", [128, 512], F32)
            apT = AL("psum_tensor", "a_pT", [128, 1024], F32)

            def mg(d, k):
                v = d.get(k)
                return [v] if v is not None else []

            for pi in range(5):
                for half in range(2):
                    for cc in range(8):
                        off = ((pi * 2 + half) * 8 + cc) * 512
                        SY(lambda s, pi=pi, half=half, cc=cc, off=off:
                           s.dma_start(out=a_w[:, off:off + 512],
                                       in_=wat_t[pi, half, cc, :, :]))
            aw_ready = (sl.h, sl.n)
            SY(lambda s: s.dma_start(out=a_lngr[:, :], in_=ln_g[:][None, :]),
               inc=s_lng)
            lgm = {}
            for half in range(2):
                bank = apA if half == 0 else apB
                q1 = TE(lambda t, half=half, bank=bank: t.matmul(
                    bank[:, :], ones_f[:, :],
                    a_lngr[0:1, half * 512:(half + 1) * 512],
                    start=True, stop=True), waits=[(s_lng.h, s_lng.n)])
                lgm[half] = VE(lambda v, half=half, bank=bank: v.tensor_copy(
                    out=a_lngb[:, half * 512:(half + 1) * 512], in_=bank[:, :]),
                    waits=[q1])
            stm = GP(lambda g: g.memset(a_st[:, :], 0.0))

            xln_m = {}
            xltr_m = {}
            xlte_m = {}
            pmm = {}
            pev = {}
            fm_m = {}
            qst_m = {}
            kn_m = {}
            vn_m = {}
            smm = {}
            sadd = {}

            def proj_mm(j, pi, half, bank, bwar):
                for cc in range(8):
                    off = ((pi * 2 + half) * 8 + cc) * 512
                    TE(lambda t, cc=cc, off=off, bank=bank: t.matmul(
                        bank[:, :], a_xlnT[:, cc * 128:(cc + 1) * 128],
                        a_w[:, off:off + 512], start=(cc == 0), stop=False),
                        inc=None,
                        waits=([xlte_m[j], aw_ready] + bwar) if cc == 0 else ())
                return TE(lambda t, pi=pi, half=half, bank=bank: t.matmul(
                    bank[:, :], ones_b[:, :],
                    battn[0:1, pi * D + half * 512:pi * D + (half + 1) * 512],
                    start=False, stop=True))

            def fmap_norm(j, dst_bf, nrmoff, evm, extra_dst_war, sl_=0):
                fsl = a_f[:, sl_ * D:(sl_ + 1) * D]
                f1 = VE(lambda v, fsl=fsl: v.tensor_scalar(
                    out=fsl, in0=fsl, scalar1=0.5, scalar2=None,
                    op0=ALU.add), waits=[evm])
                f2 = VE(lambda v, fsl=fsl: v.tensor_tensor(
                    out=a_sq[:, :], in0=fsl, in1=fsl, op=ALU.mult),
                    waits=[f1])
                nr = a_nrm[:, nrmoff:nrmoff + H]
                f3 = VE(lambda v, nr=nr: v.tensor_reduce(
                    out=nr, in_=bass.AP(a_sq, 0, [[D, 128], [DH, H], [1, DH]]),
                    axis=AX.X, op=ALU.add), waits=[f2])
                f4 = SC(lambda s, nr=nr: s.activation(out=nr, in_=nr,
                                                      func=AF.Sqrt), waits=[f3])
                f5 = VE(lambda v, nr=nr: v.tensor_scalar(
                    out=nr, in0=nr, scalar1=1e-6, scalar2=None, op0=ALU.add),
                    waits=[f4])
                f6 = VE(lambda v, nr=nr: v.reciprocal(out=nr, in_=nr),
                        waits=[f5])
                last = None
                for h in range(H):
                    last = VE(lambda v, h=h, nr=nr, fsl=fsl: v.tensor_scalar(
                        out=dst_bf[:, h * DH:(h + 1) * DH],
                        in0=fsl[:, h * DH:(h + 1) * DH],
                        scalar1=nr[:, h:h + 1], scalar2=None, op0=ALU.mult),
                        waits=[f6] + (extra_dst_war if h == 0 else []))
                return last

            for j in range(NT):
                pb = 24 * (j % 2)
                xslot = a_xt[:, (j % 2) * D:(j % 2) * D + D]
                xm = SY(lambda s, j=j, xslot=xslot: s.dma_start(
                    out=xslot, in_=xs2[j * 128:(j + 1) * 128, :]),
                    inc=r_axl[j % 2], waits=mg(xln_m, j - 2))
                cs = stat[:, pb + 0:pb + 1]
                cq = stat[:, pb + 1:pb + 2]
                cm_ = stat[:, pb + 2:pb + 3]
                cvv = stat[:, pb + 3:pb + 4]
                ctmp = stat[:, pb + 4:pb + 5]
                crq = stat[:, pb + 5:pb + 6]
                rs_ = stat[:, pb + 6:pb + 7]
                nb_ = stat[:, pb + 7:pb + 8]
                colwar = mg(xln_m, j - 2)
                s1 = SC(lambda s, xslot=xslot, cs=cs: s.activation(
                    out=a_xln[:, :], in_=xslot, func=AF.Identity, accum_out=cs),
                    waits=[xm] + mg(xltr_m, j - 1) + colwar)
                s2 = SC(lambda s, xslot=xslot, cq=cq: s.activation(
                    out=a_xln[:, :], in_=xslot, func=AF.Square, accum_out=cq),
                    waits=[s1])
                v1 = VE(lambda v, cs=cs, cm_=cm_: v.tensor_scalar(
                    out=cm_, in0=cs, scalar1=1.0 / D, scalar2=None,
                    op0=ALU.mult), waits=[s1] + colwar)
                v2 = VE(lambda v, cq=cq, cvv=cvv: v.tensor_scalar(
                    out=cvv, in0=cq, scalar1=1.0 / D, scalar2=None,
                    op0=ALU.mult), waits=[s2])
                v3 = VE(lambda v, cm_=cm_: v.tensor_tensor(
                    out=a_xln[:, 0:1], in0=cm_, in1=cm_, op=ALU.mult),
                    waits=[v1, s2])
                v4 = VE(lambda v, cvv=cvv: v.tensor_tensor(
                    out=cvv, in0=cvv, in1=a_xln[:, 0:1], op=ALU.subtract),
                    waits=[v2, v3])
                r1 = VE(lambda v, cvv=cvv, ctmp=ctmp: v.tensor_scalar(
                    out=ctmp, in0=cvv, scalar1=1e-5, scalar2=None, op0=ALU.add),
                    waits=[v4])
                r2 = VE(lambda v, ctmp=ctmp, crq=crq: v.reciprocal(
                    out=crq, in_=ctmp), waits=[r1])
                r3 = SC(lambda s, crq=crq, rs_=rs_: s.activation(
                    out=rs_, in_=crq, func=AF.Sqrt), waits=[r2])
                n1 = VE(lambda v, cm_=cm_, rs_=rs_, nb_=nb_: v.tensor_tensor(
                    out=nb_, in0=cm_, in1=rs_, op=ALU.mult), waits=[r3] + colwar)
                n2 = VE(lambda v, nb_=nb_: v.tensor_scalar(
                    out=nb_, in0=nb_, scalar1=-1.0, scalar2=None, op0=ALU.mult),
                    waits=[n1])
                xln_m[j] = SC(lambda s, xslot=xslot, rs_=rs_, nb_=nb_:
                              s.activation(
                    out=a_xln[:, :], in_=xslot, func=AF.Identity, scale=rs_,
                    bias=nb_), waits=[n2, v3])
                for cc in range(8):
                    bank = apC if cc < 4 else apD
                    q = cc % 4
                    xltr_m[j] = TE(lambda t, cc=cc, bank=bank, q=q: t.transpose(
                        out=bank[:, q * 128:(q + 1) * 128],
                        in_=a_xln[:, cc * 128:(cc + 1) * 128],
                        identity=ident_f[:, :]),
                        waits=([xln_m[j]] + mg(xlte_m, j - 1)
                               + mg(smm, (j - 1, 1, 7)))
                        if cc == 0 else [xln_m[j]])
                for cc in range(8):
                    bank = apC if cc < 4 else apD
                    q = cc % 4
                    xlte_m[j] = SC(lambda s, cc=cc, bank=bank, q=q: s.activation(
                        out=a_xlnT[:, cc * 128:(cc + 1) * 128],
                        in_=bank[:, q * 128:(q + 1) * 128], func=AF.Identity),
                        waits=[xltr_m[j]]
                        + (mg(pmm, (j - 1, 4, 1)) if cc == 0 else []))
                for half in range(2):
                    bank = apA if half == 0 else apB
                    pmm[(j, 0, half)] = proj_mm(
                        j, 0, half, bank, mg(pev, (j - 1, 4, half)))
                    pev[(j, 0, half)] = SC(lambda s, half=half, bank=bank:
                                           s.activation(
                        out=a_f[:, half * 512:(half + 1) * 512], in_=bank[:, :],
                        func=AF.Silu),
                        waits=[pmm[(j, 0, half)]]
                        + (mg(fm_m, (j - 1, 2)) if half == 0 else []))
                fm_m[(j, 0)] = fmap_norm(
                    j, a_qn, 0, pev[(j, 0, 1)],
                    mg(qst_m, j - 1) + [pev[(j, 0, 0)]], sl_=0)
                qst_m[j] = GP(lambda g, j=j: g.dma_start(
                    out=qn_hbm[j * 128:(j + 1) * 128, :], in_=a_qn[:, :]),
                    inc=sw, k=16, waits=[fm_m[(j, 0)]])
                for kk in range(2):
                    for half in range(2):
                        bank = apA if half == 0 else apB
                        pmm[(j, 1 + kk, half)] = proj_mm(
                            j, 1 + kk, half, bank,
                            [pev[(j, 2 + kk, half)]] if kk > 0 else
                            [pev[(j, 0, half)]])
                        ksl = (1 + kk) % 2
                        kwar = ([fm_m[(j, 0)]] if kk == 1
                                else mg(fm_m, (j - 1, 1)))
                        pev[(j, 1 + kk, half)] = SC(
                            lambda s, half=half, bank=bank, ksl=ksl:
                            s.activation(
                                out=a_f[:, ksl * D + half * 512:
                                        ksl * D + (half + 1) * 512],
                                in_=bank[:, :], func=AF.Silu),
                            waits=[pmm[(j, 1 + kk, half)]]
                            + (kwar if half == 0 else []))
                    fm_m[(j, 1 + kk)] = fmap_norm(
                        j, a_kn, H, pev[(j, 1 + kk, 1)],
                        mg(smm, (j, kk - 1, 7)) + mg(smm, (j - 1, 1, 7))
                        + [pev[(j, 1 + kk, 0)]], sl_=(1 + kk) % 2)
                    kn_m[(j, kk)] = fm_m[(j, 1 + kk)]
                    for half in range(2):
                        bank = apA if half == 0 else apB
                        pmm[(j, 3 + kk, half)] = proj_mm(
                            j, 3 + kk, half, bank,
                            [pev[(j, 1 + kk, half)]])
                        pev[(j, 3 + kk, half)] = SC(
                            lambda s, half=half, bank=bank, kk=kk: s.activation(
                                out=a_vn[:, half * 512:(half + 1) * 512],
                                in_=bank[:, :], func=AF.Identity),
                            waits=[pmm[(j, 3 + kk, half)],
                                   fm_m[(j, 1 + kk)]]
                            + mg(smm, (j, kk - 1, 7)) + mg(smm, (j - 1, 1, 7)))
                    vn_m[(j, kk)] = pev[(j, 3 + kk, 1)]
                    for hp in range(8):
                        bank = apC if hp % 2 == 0 else apD
                        if hp >= 2:
                            swar = mg(sadd, (j, kk, hp - 2))
                        elif kk > 0:
                            swar = mg(sadd, (j, kk - 1, 6 + hp))
                        else:
                            swar = mg(sadd, (j - 1, 1, 6 + hp))
                        smm[(j, kk, hp)] = TE(lambda t, hp=hp, bank=bank:
                                              t.matmul(
                            bank[:, 0:128], a_kn[:, hp * 128:(hp + 1) * 128],
                            a_vn[:, hp * 128:(hp + 1) * 128],
                            start=True, stop=True),
                            waits=[kn_m[(j, kk)], vn_m[(j, kk)]] + swar
                            + ([xltr_m[j]] if hp == 0 and kk == 0 else []))
                        sadd[(j, kk, hp)] = VE(lambda v, kk=kk, hp=hp,
                                               bank=bank: v.tensor_tensor(
                            out=a_st[:, kk * 1024 + hp * 128:
                                     kk * 1024 + (hp + 1) * 128],
                            in0=a_st[:, kk * 1024 + hp * 128:
                                     kk * 1024 + (hp + 1) * 128],
                            in1=bank[:, 0:128], op=ALU.add),
                            waits=[smm[(j, kk, hp)], (sg.h, stm[1])]
                            + mg(sadd, (j - 1, kk, hp)))

            sf1 = VE(lambda v: v.tensor_tensor(
                out=a_st[:, 0:1024], in0=a_st[:, 0:1024],
                in1=a_st[:, 1024:2048], op=ALU.mult),
                waits=[sadd[(NT - 1, 0, 7)], sadd[(NT - 1, 1, 7)],
                       sadd[(NT - 1, 0, 6)], sadd[(NT - 1, 1, 6)]])
            sf2 = VE(lambda v: v.tensor_scalar(
                out=a_st[:, 0:1024], in0=a_st[:, 0:1024], scalar1=scale2,
                scalar2=None, op0=ALU.mult), waits=[sf1])
            scp = {}
            for h in range(H):
                hp, sub = h // 2, h % 2
                scp[h] = VE(lambda v, hp=hp, sub=sub: v.tensor_copy(
                    out=a_S[sub * 64:sub * 64 + 64, hp * 64:(hp + 1) * 64],
                    in_=a_st[sub * 64:sub * 64 + 64,
                             hp * 128 + sub * 64:hp * 128 + sub * 64 + 64]),
                    waits=[sf2])
            wo_first = SY(lambda s: s.dma_start(
                out=a_w[:, 0:512], in_=wo_t[0, 0, :, :]),
                waits=[pmm[(NT - 1, 4, 1)]])
            for half in range(2):
                for cc in range(8):
                    if half == 0 and cc == 0:
                        continue
                    SY(lambda s, half=half, cc=cc: s.dma_start(
                        out=a_w[:, (half * 8 + cc) * 512:
                                (half * 8 + cc + 1) * 512],
                        in_=wo_t[half, cc, :, :]))
            wo_ready = (sl.h, sl.n)

            for j in range(NT):
                pb = 24 * (j % 2)
                SY(lambda s, j=j: s.dma_start(
                    out=a_qn[:, :], in_=qn_hbm[j * 128:(j + 1) * 128, :]),
                   inc=r_aqn[0], waits=W(st, sv, sw))
                for hp in range(8):
                    TE(lambda t, hp=hp: t.transpose(
                        out=apT[:, hp * 128:(hp + 1) * 128],
                        in_=a_qn[:, hp * 128:(hp + 1) * 128],
                        identity=ident_f[:, :]),
                       waits=(W(st, sv) + W(r_aqn[0])) if hp == 0 else ())
                VE(lambda v: v.tensor_copy(out=a_qT[:, :], in_=apT[:, 0:D]),
                   waits=W(st, sv))
                for h in range(H):
                    bank = apA if h < 8 else apB
                    sub = h % 2
                    TE(lambda t, h=h, bank=bank, sub=sub: t.matmul(
                        bank[:, (h % 8) * 64:(h % 8 + 1) * 64],
                        a_qT[sub * 64:sub * 64 + 64,
                             (h // 2) * 128:(h // 2 + 1) * 128],
                        a_S[sub * 64:sub * 64 + 64, (h // 2) * 64:(h // 2 + 1) * 64],
                        start=True, stop=True), waits=W(sv, st))
                VE(lambda v: v.tensor_copy(out=a_og[:, 0:512], in_=apA[:, :]),
                   waits=W(st, sv))
                VE(lambda v: v.tensor_copy(out=a_og[:, 512:1024], in_=apB[:, :]),
                   waits=W(st, sv))
                cs = stat[:, pb + 0:pb + 1]
                cq = stat[:, pb + 1:pb + 2]
                cm_ = stat[:, pb + 2:pb + 3]
                cv = stat[:, pb + 3:pb + 4]
                ln_stats(a_og[:, :], a_f[:, 0:D], cs, cq, cm_, cv)
                ctmp = stat[:, pb + 4:pb + 5]
                crq = stat[:, pb + 5:pb + 6]
                rs_ = stat[:, pb + 6:pb + 7]
                nb_ = stat[:, pb + 7:pb + 8]
                rsqrt_eps(cv, 1e-5, ctmp, crq, rs_)
                VE(lambda v, nb_=nb_, cm_=cm_, rs_=rs_: v.tensor_tensor(
                    out=nb_, in0=cm_, in1=rs_, op=ALU.mult), waits=W(ss, sv))
                VE(lambda v, nb_=nb_: v.tensor_scalar(out=nb_, in0=nb_, scalar1=-1.0,
                                                      scalar2=None, op0=ALU.mult),
                   waits=W(sv))
                SC(lambda s, rs_=rs_, nb_=nb_: s.activation(
                    out=a_og[:, :], in_=a_og[:, :], func=AF.Identity, scale=rs_,
                    bias=nb_), waits=W(sv, ss))
                VE(lambda v: v.tensor_tensor(out=a_og[:, :], in0=a_og[:, :],
                                             in1=a_lngb[:, :], op=ALU.mult),
                   waits=W(ss, sv, sg))
                for q in range(8):
                    bank = apC if q < 4 else apD
                    TE(lambda t, q=q, bank=bank: t.transpose(
                        out=bank[:, (q % 4) * 128:(q % 4 + 1) * 128],
                        in_=a_og[:, q * 128:(q + 1) * 128],
                        identity=ident_f[:, :]), waits=W(sv, st, ss))
                for q in range(8):
                    bank = apC if q < 4 else apD
                    SC(lambda s, q=q, bank=bank: s.activation(
                        out=a_ogT[:, q * 128:(q + 1) * 128],
                        in_=bank[:, (q % 4) * 128:(q % 4 + 1) * 128],
                        func=AF.Identity), waits=W(st, ss))
                xslot = a_xt[:, (j % 2) * D:(j % 2) * D + D]
                SY(lambda s, j=j, xslot=xslot: s.dma_start(
                    out=xslot, in_=xs2[j * 128:(j + 1) * 128, :]),
                   inc=r_axl[j % 2], waits=W(sv))
                for half in range(2):
                    bank = apA if half == 0 else apB
                    for cc in range(8):
                        TE(lambda t, half=half, cc=cc, bank=bank: t.matmul(
                            bank[:, :], a_ogT[:, cc * 128:(cc + 1) * 128],
                            a_w[:, (half * 8 + cc) * 512:(half * 8 + cc + 1) * 512],
                            start=(cc == 0), stop=False), inc=None,
                           waits=(([wo_ready] + W(ss, sv, st)) if cc == 0 else ()))
                    TE(lambda t, half=half, bank=bank: t.matmul(
                        bank[:, :], ones_b[:, :],
                        bwo[0:1, half * 512:(half + 1) * 512],
                        start=False, stop=True))
                    VE(lambda v, half=half, bank=bank, xslot=xslot, j=j:
                       v.tensor_tensor(
                           out=a_xo[:, half * 512:(half + 1) * 512],
                           in0=bank[:, :],
                           in1=xslot[:, half * 512:(half + 1) * 512], op=ALU.add),
                       waits=W(st, sv, sw) + W(r_axl[j % 2]))
                GP(lambda g, j=j: g.dma_start(
                    out=xs3[j * 128:(j + 1) * 128, :], in_=a_xo[:, :]),
                   inc=sw, k=16, waits=W(sv))
            pa.run()

    if "F2" in phases:
        phase_ffn(xs3, y_out, "f2")

    if debug_out is not None:
        dbg = new_prog()
        srcs = {"xs1": xs1, "xs2": xs2, "xs3": xs3, "qn": qn_hbm}
        dsrc = srcs[debug_out]
        for j in range(NT):
            GP(lambda g, j=j: g.dma_start(
                out=y_out[j * 128:(j + 1) * 128, :],
                in_=dsrc[j * 128:(j + 1) * 128, :]), inc=sw, k=16)
        dbg.run()

    fin = new_prog()
    fin.run()

    es.close()
    return nc


def make_inputs(inputs, T=NSEQ_FULL):
    x = np.ascontiguousarray(np.asarray(inputs["L0_attn_a_hidden_states"],
                                        dtype=np.float32))
    base = {
        "norm_w": inputs["L0_attn_a_norm_w"],
        "W_in": inputs["L0_attn_a_W_in"],
        "conv_w": np.asarray(inputs["L0_attn_a_conv_w"]).reshape(E, KC),
        "W_x": inputs["L0_attn_a_W_x"],
        "W_dt": inputs["L0_attn_a_W_dt"],
        "b_dt": inputs["L0_attn_a_b_dt"],
        "A_log": inputs["L0_attn_a_A_log"],
        "D_param": inputs["L0_attn_a_D_param"],
        "W_out": inputs["L0_attn_a_W_out"],
        "WqT": np.asarray(inputs["L0_attn_b_Wq"]).T,
        "bq": inputs["L0_attn_b_bq"],
        "WkT": np.transpose(np.asarray(inputs["L0_attn_b_Wk"]), (0, 2, 1)),
        "bk": inputs["L0_attn_b_bk"],
        "WvT": np.transpose(np.asarray(inputs["L0_attn_b_Wv"]), (0, 2, 1)),
        "bv": inputs["L0_attn_b_bv"],
        "WoT": np.asarray(inputs["L0_attn_b_Wo"]).T,
        "bo": inputs["L0_attn_b_bo"],
        "ln_g": inputs["L0_attn_b_ln_g"],
        "ln_b": inputs["L0_attn_b_ln_b"],
        "wgT": np.asarray(inputs["L0_ffn_w_gate"]).T,
        "wuT": np.asarray(inputs["L0_ffn_w_up"]).T,
        "wdT": np.asarray(inputs["L0_ffn_w_down"]).T,
    }
    base = {k: np.ascontiguousarray(np.asarray(v, dtype=np.float32))
            for k, v in base.items()}
    return [dict(base, x0=np.ascontiguousarray(x[b, :T])) for b in range(x.shape[0])]


def kernel(**inputs):
    nc = build(NSEQ_FULL)
    in_maps = make_inputs(inputs, NSEQ_FULL)
    res = run_bass_kernel_spmd(nc, in_maps, core_ids=list(range(8)))
    return np.stack([np.asarray(r["y"], dtype=np.float32) for r in res.results],
                    axis=0)


# version 23
# speedup vs baseline: 1.0090x; 1.0090x over previous
import numpy as np
from contextlib import ExitStack

import concourse.bass as bass
import concourse.mybir as mybir
from concourse.bass_utils import run_bass_kernel_spmd

F32 = mybir.dt.float32
BF16 = mybir.dt.bfloat16
F8 = mybir.dt.float8e4
I32 = mybir.dt.int32
AF = mybir.ActivationFunctionType
ALU = mybir.AluOpType
AX = mybir.AxisListType

D = 1024
E = 2048
NS = 16
RDT = 64
KC = 4
INNER = 4096
H = 16
DH = 64
NE = E // 128
NSEQ_FULL = 4096


def build(T=NSEQ_FULL, phases=("M", "F1", "A", "F2"), debug_out=None, a_stages=2):
    NT = T // 128
    NCH = T // 512
    nc = bass.Bass("TRN2", target_bir_lowering=False)
    es = ExitStack()

    def din(name, shape, dt=F32):
        return nc.dram_tensor(name, list(shape), dt, kind="ExternalInput").ap()

    x0 = din("x0", (T, D))
    norm_w = din("norm_w", (D,))
    W_in = din("W_in", (D, 2 * E))
    conv_w = din("conv_w", (E, KC))
    W_x = din("W_x", (E, RDT + 2 * NS))
    W_dt = din("W_dt", (RDT, E))
    b_dt = din("b_dt", (E,))
    A_log = din("A_log", (E, NS))
    D_param = din("D_param", (E,))
    W_out = din("W_out", (E, D))
    WqT = din("WqT", (D, D))
    bq = din("bq", (D,))
    WkT = din("WkT", (2, D, D))
    bk = din("bk", (2, D))
    WvT = din("WvT", (2, D, D))
    bv = din("bv", (2, D))
    WoT = din("WoT", (D, D))
    bo = din("bo", (D,))
    ln_g = din("ln_g", (D,))
    ln_b = din("ln_b", (D,))
    wgT = din("wgT", (D, INNER))
    wuT = din("wuT", (D, INNER))
    wdT = din("wdT", (INNER, D))

    y_out = nc.dram_tensor("y", [T, D], F32, kind="ExternalOutput").ap()

    def scr(name, shape, dt):
        return nc.dram_tensor(name, list(shape), dt, kind="Internal").ap()

    xs1 = scr("xs1", (T, D), F32)
    xs2 = scr("xs2", (T, D), F32)
    xs3 = scr("xs3", (T, D), F32)
    qn_hbm = scr("qn_hbm", (T, D), F32)
    z_hbm = scr("z_hbm", (E, T), BF16)
    wbin_t = scr("wbin_t", (2 * NE, 8, 128, 128), BF16)
    wbout_t = scr("wbout_t", (2, NE, 128, 512), BF16)
    wat_t = scr("wat_t", (5, 2, 8, 128, 512), BF16)
    wo_t = scr("wo_t", (2, 8, 128, 512), BF16)
    wg8 = scr("wg8", (8, 8, 128, 512), F8)
    wu8 = scr("wu8", (8, 8, 128, 512), F8)
    wd8 = scr("wd8", (32, 2, 128, 512), F8)

    def psb(name, shape, dt=F32):
        return es.enter_context(nc.sbuf_tensor(name, list(shape), dt))

    ident_f = psb("ident_f", (128, 128))
    ident_b = psb("ident_b", (128, 128), BF16)
    ones_b = psb("ones_b", (1, 128), BF16)
    ones_col = psb("ones_col", (128, 1))
    ones_f = psb("ones_f", (1, 128))
    cbias = psb("cbias", (128, 3))
    A_sb = psb("A_sb", (128, NE * NS))
    conv_sb = psb("conv_sb", (128, NE * KC))
    D_sb = psb("D_sb", (128, NE))
    bdt_sb = psb("bdt_sb", (128, NE))
    norm_sb = psb("norm_sb", (128, 8))
    battn = psb("battn", (1, 5 * D), BF16)
    bwo = psb("bwo", (1, D), BF16)
    rswg = psb("rswg", (128, 4))
    swcol = psb("swcol", (128, 4))
    stat = psb("stat", (128, 64))

    all_sems = []

    def mksem(name):
        class _S:
            pass
        s = _S()
        s.h = es.enter_context(nc.semaphore(name))
        s.n = 0
        all_sems.append(s)
        return s

    sv = mksem("sv")
    ss = mksem("ss")
    st = mksem("st")
    sg = mksem("sg")
    sl = mksem("sl")
    sw = mksem("sw")
    sl2 = mksem("sl2")

    def mkring(name, k):
        return [mksem(name + str(i)) for i in range(k)]

    r_mxl = mkring("r_mxl", 2)
    r_win = mkring("r_win", 2)
    r_wou = mkring("r_wou", 2)
    r_zrl = mkring("r_zrl", 4)
    r_mxr = mkring("r_mxr", 1)
    r_zbc = mkring("r_zbc", 2)
    r_zst = mkring("r_zst", 2)
    r_fxl = mkring("r_fxl", 2)
    r_fwd = mkring("r_fwd", 2)
    r_axl = mkring("r_axl", 2)
    r_aqn = mkring("r_aqn", 1)
    s_wt = mksem("s_wt")
    s_wt2 = mksem("s_wt2")
    sw1 = mksem("sw1")
    s_dst = mksem("s_dst")
    s_cst = mksem("s_cst")
    s_woa = mksem("s_woa")
    s_lng = mksem("s_lng")

    cur = [None]

    def _emit(engine, make, inc, k, waits):
        best = {}
        for wv in waits:
            if wv is None:
                continue
            h, v = wv
            if v > 0 and v > best.get(id(h), (None, -1))[1]:
                best[id(h)] = (h, v)
        waits = list(best.values())
        if inc is not None:
            inc.n += k
        sem = inc.h if inc is not None else None

        def fn(eng, waits=waits, make=make, sem=sem, k=k):
            for h, v in waits:
                eng.wait_ge(h, v)
            instr = make(eng)
            if sem is not None:
                instr.then_inc(sem, k)
        cur[0].emit(engine, fn)
        return (inc.h, inc.n) if inc is not None else None

    def SY(make, inc=sl, k=16, waits=()):
        return _emit("sync", make, inc, k, waits)

    def SD(make, inc=sl2, k=16, waits=()):
        return _emit("scalar", make, inc, k, waits)

    def GP(make, inc=sg, k=1, waits=()):
        return _emit("gpsimd", make, inc, k, waits)

    def VE(make, inc=sv, k=1, waits=()):
        return _emit("vector", make, inc, k, waits)

    def SC(make, inc=ss, k=1, waits=()):
        return _emit("scalar", make, inc, k, waits)

    def TE(make, inc=st, k=1, waits=()):
        return _emit("tensor", make, inc, k, waits)

    def W(*sems):
        return [(s.h, s.n) for s in sems]

    def MK(s):
        return (s.h, s.n)

    class Prog:
        ENGINES = ("sync", "vector", "scalar", "tensor", "gpsimd")

        def __init__(self):
            self.streams = {e: [] for e in self.ENGINES}

        def emit(self, engine, fn):
            self.streams[engine].append(fn)

        def barrier(self):
            waits = W(*all_sems)
            for e in self.ENGINES:
                def fn(eng, waits=waits):
                    for h, v in waits:
                        if v > 0:
                            eng.wait_ge(h, v)
                self.emit(e, fn)

        def run(self):
            with nc.Block() as blk:
                @blk.sync
                def _(eng, fns=self.streams["sync"]):
                    for f in fns:
                        f(eng)

                @blk.tensor
                def _(eng, fns=self.streams["tensor"]):
                    for f in fns:
                        f(eng)

                @blk.scalar
                def _(eng, fns=self.streams["scalar"]):
                    for f in fns:
                        f(eng)

                @blk.vector
                def _(eng, fns=self.streams["vector"]):
                    for f in fns:
                        f(eng)

                @blk.gpsimd
                def _(eng, fns=self.streams["gpsimd"]):
                    for f in fns:
                        f(eng)

    def new_prog():
        p = Prog()
        cur[0] = p
        p.barrier()
        return p

    def ln_stats(src_ap, dump_ap, cs, cq, cm_, cv, rdy=None):
        rw = W(rdy) if rdy is not None else []
        SC(lambda s: s.activation(out=dump_ap, in_=src_ap, func=AF.Identity,
                                  accum_out=cs), waits=rw + W(ss, sv))
        SC(lambda s: s.activation(out=dump_ap, in_=src_ap, func=AF.Square,
                                  accum_out=cq), waits=W(ss))
        VE(lambda v: v.tensor_scalar(out=cm_, in0=cs, scalar1=1.0 / D,
                                     scalar2=None, op0=ALU.mult), waits=W(ss, sv))
        VE(lambda v: v.tensor_scalar(out=cv, in0=cq, scalar1=1.0 / D,
                                     scalar2=None, op0=ALU.mult), waits=W(sv))
        tmpc = dump_ap[:, 0:1]
        VE(lambda v: v.tensor_tensor(out=tmpc, in0=cm_, in1=cm_, op=ALU.mult),
           waits=W(sv, ss))
        VE(lambda v: v.tensor_tensor(out=cv, in0=cv, in1=tmpc, op=ALU.subtract),
           waits=W(sv))

    def rsqrt_eps(cv, eps, cq, crq, crs):
        VE(lambda v: v.tensor_scalar(out=cq, in0=cv, scalar1=eps, scalar2=None,
                                     op0=ALU.add), waits=W(sv))
        VE(lambda v: v.reciprocal(out=crq, in_=cq), waits=W(sv))
        SC(lambda s: s.activation(out=crs, in_=crq, func=AF.Sqrt), waits=W(sv, ss))

    prep = new_prog()
    with nc.sbuf_tensor("p_itmp", [128, 128], F32) as itmp, \
         nc.sbuf_tensor("p_icol", [128, 1], F32) as icol, \
         nc.sbuf_tensor("p_wt", [128, 8192], F32) as wt, \
         nc.sbuf_tensor("p_wb", [128, 8192], BF16) as wb, \
         nc.sbuf_tensor("p_wi", [128, 8192], F32) as wi, \
         nc.sbuf_tensor("p_w8", [128, 8192], F8) as w8, \
         nc.sbuf_tensor("p_col", [128, 4], F32) as pcol, \
         nc.sbuf_tensor("p_row", [1, 16], F32) as prow, \
         nc.sbuf_tensor("p_brow", [1, 5 * D], F32) as brow, \
         nc.sbuf_tensor("p_lnb", [128, 8], F32) as lnbc, \
         nc.sbuf_tensor("p_bo", [1, D], F32) as borow, \
         nc.sbuf_tensor("p_wo", [128, 16 * 512], F32) as wochunk, \
         nc.sbuf_tensor("p_bwo", [1, D], F32) as bwof, \
         nc.psum_tensor("p_ps", [128, 512], F32) as pps:

        GP(lambda g: g.iota(itmp[:, :], [[1, 128]], channel_multiplier=0,
                            allow_small_or_imprecise_dtypes=True))
        GP(lambda g: g.iota(icol[:, :], [[0, 1]], channel_multiplier=1,
                            allow_small_or_imprecise_dtypes=True))
        GP(lambda g: g.memset(ones_b[:, :], 1.0))
        GP(lambda g: g.memset(ones_col[:, :], 1.0))
        GP(lambda g: g.memset(ones_f[:, :], 1.0))
        GP(lambda g: g.memset(cbias[:, 0:1], 12582912.0))
        GP(lambda g: g.memset(cbias[:, 1:2], -12582912.0))
        GP(lambda g: g.memset(cbias[:, 2:3], 1.0))
        VE(lambda v: v.tensor_scalar(out=ident_f[:, :], in0=itmp[:, :],
                                     scalar1=icol[:, :], scalar2=None,
                                     op0=ALU.is_equal), waits=W(sg))
        VE(lambda v: v.tensor_copy(out=ident_b[:, :], in_=ident_f[:, :]),
           waits=W(sv))

        for et in range(NE):
            SY(lambda s, et=et: s.dma_start(out=A_sb[:, et * NS:(et + 1) * NS],
                                            in_=A_log[et * 128:(et + 1) * 128, :]),
               inc=s_cst)
            SY(lambda s, et=et: s.dma_start(out=conv_sb[:, et * KC:(et + 1) * KC],
                                            in_=conv_w[et * 128:(et + 1) * 128, :]),
               inc=s_cst)
            SY(lambda s, et=et: s.dma_start(
                out=D_sb[:, et:et + 1], in_=D_param[et * 128:(et + 1) * 128][:, None]),
               inc=s_cst)
            SY(lambda s, et=et: s.dma_start(
                out=bdt_sb[:, et:et + 1], in_=b_dt[et * 128:(et + 1) * 128][:, None]),
               inc=s_cst)
        for cc in range(8):
            SY(lambda s, cc=cc: s.dma_start(
                out=norm_sb[:, cc:cc + 1], in_=norm_w[cc * 128:(cc + 1) * 128][:, None]),
               inc=s_cst)
        SY(lambda s: s.dma_start(out=brow[0:1, 0:D], in_=bq[:][None, :]), inc=s_cst)
        SY(lambda s: s.dma_start(out=brow[0:1, D:2 * D], in_=bk[0, :][None, :]), inc=s_cst)
        SY(lambda s: s.dma_start(out=brow[0:1, 2 * D:3 * D], in_=bk[1, :][None, :]), inc=s_cst)
        SY(lambda s: s.dma_start(out=brow[0:1, 3 * D:4 * D], in_=bv[0, :][None, :]), inc=s_cst)
        SY(lambda s: s.dma_start(out=brow[0:1, 4 * D:5 * D], in_=bv[1, :][None, :]), inc=s_cst)
        for cc in range(8):
            SY(lambda s, cc=cc: s.dma_start(
                out=lnbc[:, cc:cc + 1], in_=ln_b[cc * 128:(cc + 1) * 128][:, None]),
               inc=s_cst)
        SY(lambda s: s.dma_start(out=borow[:, :], in_=bo[:][None, :]), inc=s_cst)
        for half in range(2):
            for cc in range(8):
                SD(lambda s, cc=cc, half=half: s.dma_start(
                    out=wochunk[:, (half * 8 + cc) * 512:(half * 8 + cc + 1) * 512],
                    in_=WoT[cc * 128:(cc + 1) * 128, half * 512:(half + 1) * 512]),
                   inc=s_cst)
        SC(lambda s: s.activation(out=A_sb[:, :], in_=A_sb[:, :], func=AF.Exp),
           waits=W(s_cst))
        VE(lambda v: v.tensor_scalar(out=A_sb[:, :], in0=A_sb[:, :], scalar1=-1.0,
                                     scalar2=None, op0=ALU.mult), waits=W(ss))
        VE(lambda v: v.tensor_copy(out=battn[:, :], in_=brow[:, :]), waits=W(s_cst))
        for half in range(2):
            for cc in range(8):
                TE(lambda t, cc=cc, half=half: t.matmul(
                    pps[0:1, 0:512], lnbc[:, cc:cc + 1],
                    wochunk[:, (half * 8 + cc) * 512:(half * 8 + cc + 1) * 512],
                    start=(cc == 0), stop=(cc == 7)),
                   inc=st if cc == 7 else None,
                   waits=(W(s_cst, sv) if cc == 0 else ()))
            VE(lambda v, half=half: v.tensor_tensor(
                out=bwof[0:1, half * 512:(half + 1) * 512], in0=pps[0:1, 0:512],
                in1=borow[0:1, half * 512:(half + 1) * 512], op=ALU.add),
               waits=W(st, s_cst, sv))
        VE(lambda v: v.tensor_copy(out=bwo[:, :], in_=bwof[:, :]), waits=W(sv))

        WTW = 4096
        wt_guard = {0: None, 1: None}
        out_guard = {0: None, 1: None}
        rctr = [0]

        def ld_row(src_ap, r0, width):
            slq = rctr[0] % 2
            rctr[0] += 1
            Q = SY if slq == 0 else SD
            ld = Q(lambda s, src_ap=src_ap, r0=r0, width=width, slq=slq:
                   s.dma_start(out=wt[:, slq * WTW:slq * WTW + width],
                               in_=src_ap[r0:r0 + 128, 0:width]),
                   inc=(s_wt if slq == 0 else s_wt2), waits=[wt_guard[slq]])
            return slq, ld

        def retile(src_ap, n_cc, width, dst_fn, add_i=False):
            for cc in range(n_cc):
                slq, ld = ld_row(src_ap, cc * 128, width)
                prev = ld
                if add_i:
                    prev = VE(lambda v, cc=cc, slq=slq: v.tensor_tensor(
                        out=wt[:, slq * WTW + cc * 128:slq * WTW + (cc + 1) * 128],
                        in0=wt[:, slq * WTW + cc * 128:slq * WTW + (cc + 1) * 128],
                        in1=ident_f[:, :], op=ALU.add), waits=[ld] + W(sv))
                cp = VE(lambda v, slq=slq, width=width: v.tensor_copy(
                    out=wb[:, slq * WTW:slq * WTW + width],
                    in_=wt[:, slq * WTW:slq * WTW + width]),
                    waits=[prev, out_guard[slq]] + W(sv))
                wt_guard[slq] = cp
                out_guard[slq] = dst_fn(cc, slq, [cp])

        def win_dst(cc, slq, wl):
            mk = None
            for et in range(2 * NE):
                mk = GP(lambda g, cc=cc, et=et, slq=slq: g.dma_start(
                    out=wbin_t[et, cc, :, :],
                    in_=wb[:, slq * WTW + et * 128:slq * WTW + (et + 1) * 128]),
                    inc=(sw if slq == 0 else sw1), k=16, waits=wl)
            return mk
        retile(W_in, 8, 2 * E, win_dst)

        def wout_dst(et, slq, wl):
            mk = None
            for half in range(2):
                mk = GP(lambda g, et=et, half=half, slq=slq: g.dma_start(
                    out=wbout_t[half, et, :, :],
                    in_=wb[:, slq * WTW + half * 512:slq * WTW + (half + 1) * 512]),
                    inc=(sw if slq == 0 else sw1), k=16, waits=wl)
            return mk
        retile(W_out, NE, D, wout_dst)

        for pi, srcw in enumerate([WqT, WkT[0], WkT[1], WvT[0], WvT[1]]):
            def att_dst(cc, slq, wl, pi=pi):
                mk = None
                for half in range(2):
                    mk = GP(lambda g, cc=cc, half=half, pi=pi, slq=slq: g.dma_start(
                        out=wat_t[pi, half, cc, :, :],
                        in_=wb[:, slq * WTW + half * 512:slq * WTW + (half + 1) * 512]),
                        inc=(sw if slq == 0 else sw1), k=16, waits=wl)
                return mk
            retile(srcw, 8, D, att_dst, add_i=True)

        for cc in range(8):
            slq = rctr[0] % 2
            rctr[0] += 1
            c1 = VE(lambda v, cc=cc, slq=slq: v.tensor_copy(
                out=wb[:, slq * WTW:slq * WTW + 512],
                in_=wochunk[:, cc * 512:(cc + 1) * 512]),
                waits=[out_guard[slq]] + W(s_cst, sv))
            c2 = VE(lambda v, cc=cc, slq=slq: v.tensor_copy(
                out=wb[:, slq * WTW + 512:slq * WTW + 1024],
                in_=wochunk[:, (8 + cc) * 512:(9 + cc) * 512]), waits=W(sv))
            if wt_guard[slq] is None:
                wt_guard[slq] = c2
            mk = None
            for half in range(2):
                mk = GP(lambda g, cc=cc, half=half, slq=slq: g.dma_start(
                    out=wo_t[half, cc, :, :],
                    in_=wb[:, slq * WTW + half * 512:slq * WTW + (half + 1) * 512]),
                    inc=(sw if slq == 0 else sw1), k=16, waits=[c2])
            out_guard[slq] = mk

        def quant(src_ap, n_cc, width, colidx, dst_fn):
            VE(lambda v: v.memset(pcol[:, 0:1], 0.0), waits=W(sv, st))
            for cc in range(n_cc):
                slq, ld = ld_row(src_ap, cc * 128, width)
                rd = VE(lambda v, slq=slq, width=width: v.tensor_reduce(
                    out=pcol[:, 1:2], in_=wt[:, slq * WTW:slq * WTW + width],
                    axis=AX.X, op=ALU.add, apply_absolute_value=True),
                    waits=[ld] + W(sv))
                wt_guard[slq] = rd
                VE(lambda v: v.tensor_tensor(out=pcol[:, 0:1], in0=pcol[:, 0:1],
                                             in1=pcol[:, 1:2], op=ALU.add),
                   waits=W(sv))
            TE(lambda t: t.matmul(pps[0:1, 0:1], pcol[:, 0:1], ones_col[:, :],
                                  start=True, stop=True), waits=W(sv, st, sg))
            VE(lambda v, n_cc=n_cc, width=width: v.tensor_scalar(
                out=prow[0:1, 0:1], in0=pps[0:1, 0:1],
                scalar1=1.0 / (n_cc * 128 * width),
                scalar2=1e-5, op0=ALU.mult, op1=ALU.max), waits=W(st, sv))
            VE(lambda v: v.reciprocal(out=prow[0:1, 1:2], in_=prow[0:1, 0:1]),
               waits=W(sv))
            TE(lambda t: t.matmul(pps[:, 0:2], ones_f[:, :], prow[0:1, 0:2],
                                  start=True, stop=True), waits=W(sv, st, sg))
            VE(lambda v, colidx=colidx: v.tensor_copy(
                out=rswg[:, colidx:colidx + 1], in_=pps[:, 0:1]), waits=W(st, sv))
            VE(lambda v, colidx=colidx: v.tensor_copy(
                out=swcol[:, colidx:colidx + 1], in_=pps[:, 1:2]), waits=W(st, sv))
            for cc in range(n_cc):
                slq, ld = ld_row(src_ap, cc * 128, width)
                q1 = VE(lambda v, colidx=colidx, slq=slq, width=width:
                        v.tensor_scalar(
                    out=wi[:, slq * WTW:slq * WTW + width],
                    in0=wt[:, slq * WTW:slq * WTW + width],
                    scalar1=swcol[:, colidx:colidx + 1], scalar2=12582912.0,
                    op0=ALU.mult, op1=ALU.add), waits=[ld] + W(sv))
                wt_guard[slq] = q1
                VE(lambda v, slq=slq, width=width: v.tensor_scalar(
                    out=wi[:, slq * WTW:slq * WTW + width],
                    in0=wi[:, slq * WTW:slq * WTW + width],
                    scalar1=-12582912.0, scalar2=1.0,
                    op0=ALU.add, op1=ALU.min), waits=W(sv))
                q3 = VE(lambda v, slq=slq, width=width: v.tensor_scalar(
                    out=w8[:, slq * WTW:slq * WTW + width],
                    in0=wi[:, slq * WTW:slq * WTW + width],
                    scalar1=-1.0, scalar2=None,
                    op0=ALU.max), waits=[out_guard[slq]] + W(sv))
                out_guard[slq] = dst_fn(cc, slq, [q3])

        def wg_dst(cc, slq, wl):
            mk = None
            for ob in range(8):
                mk = GP(lambda g, cc=cc, ob=ob, slq=slq: g.dma_start(
                    out=wg8[cc, ob, :, :],
                    in_=w8[:, slq * WTW + ob * 512:slq * WTW + (ob + 1) * 512]),
                    inc=(sw if slq == 0 else sw1), k=16, waits=wl)
            return mk
        quant(wgT, 8, INNER, 0, wg_dst)

        def wu_dst(cc, slq, wl):
            mk = None
            for ob in range(8):
                mk = GP(lambda g, cc=cc, ob=ob, slq=slq: g.dma_start(
                    out=wu8[cc, ob, :, :],
                    in_=w8[:, slq * WTW + ob * 512:slq * WTW + (ob + 1) * 512]),
                    inc=(sw if slq == 0 else sw1), k=16, waits=wl)
            return mk
        quant(wuT, 8, INNER, 1, wu_dst)

        def wd_dst(oc, slq, wl):
            mk = None
            for half in range(2):
                mk = GP(lambda g, oc=oc, half=half, slq=slq: g.dma_start(
                    out=wd8[oc, half, :, :],
                    in_=w8[:, slq * WTW + half * 512:slq * WTW + (half + 1) * 512]),
                    inc=(sw if slq == 0 else sw1), k=16, waits=wl)
            return mk
        quant(wdT, 32, D, 2, wd_dst)

        prep.run()

    if "M" in phases:
        pm = new_prog()
        with ExitStack() as esp:
            AL = lambda k, n, s, d: esp.enter_context(getattr(nc, k)(n, list(s), d))
            wx_bf = AL("sbuf_tensor", "m_wxb", [128, NE * 96], BF16)
            wdt_sb = AL("sbuf_tensor", "m_wdt", [RDT, E], F32)
            m_xt = AL("sbuf_tensor", "m_xt", [128, 2 * D], F32)
            m_xn = AL("sbuf_tensor", "m_xn", [128, D], F32)
            m_xnT = AL("sbuf_tensor", "m_xnT", [128, 8 * 512], BF16)
            m_win = AL("sbuf_tensor", "m_win", [128, 2 * 1024], BF16)
            m_stage = AL("sbuf_tensor", "m_stage", [128, 2 * 516], F32)
            m_cv = AL("sbuf_tensor", "m_cv", [128, 2 * 512], F32)
            m_u = AL("sbuf_tensor", "m_u", [128, 2 * NE * 512], BF16)
            m_szs = AL("sbuf_tensor", "m_szs", [128, 2 * 512], BF16)
            m_szl = AL("sbuf_tensor", "m_szl", [128, 4 * 512], BF16)
            m_x96 = AL("sbuf_tensor", "m_x96", [96, 2 * 512], F32)
            m_bcr = AL("sbuf_tensor", "m_bcr", [1, 1024], F32)
            m_bcb = AL("sbuf_tensor", "m_bcb", [1, 1024], BF16)
            m_Bb = AL("sbuf_tensor", "m_Bb", [128, NS * 512], BF16)
            m_Cb = AL("sbuf_tensor", "m_Cb", [128, NS * 512], BF16)
            m_delta = AL("sbuf_tensor", "m_delta", [128, 2 * 512], F32)
            m_sp = AL("sbuf_tensor", "m_sp", [128, 2 * 512], F32)
            m_du = AL("sbuf_tensor", "m_du", [128, 2 * 512], BF16)
            m_dA = AL("sbuf_tensor", "m_dA", [128, 4 * 512], F32)
            m_xx = AL("sbuf_tensor", "m_xx", [128, 2 * 2048], BF16)
            m_h = AL("sbuf_tensor", "m_h", [128, NS * 512], BF16)
            m_hcar = AL("sbuf_tensor", "m_hcar", [128, NE * NS], F32)
            m_ccar = AL("sbuf_tensor", "m_ccar", [128, NE * (KC - 1)], F32)
            m_y = AL("sbuf_tensor", "m_y", [128, 2 * 512], F32)
            m_ypr = AL("sbuf_tensor", "m_ypr", [128, NE * 512], BF16)
            m_wo = AL("sbuf_tensor", "m_wo", [128, 2 * 512], BF16)
            m_xr = AL("sbuf_tensor", "m_xr", [128, 512], F32)
            m_xo = AL("sbuf_tensor", "m_xo", [128, 512], F32)
            pA = AL("psum_tensor", "m_pA", [128, 512], F32)
            pB = AL("psum_tensor", "m_pB", [128, 512], F32)
            pC = AL("psum_tensor", "m_pC", [128, 512], F32)
            pD = AL("psum_tensor", "m_pD", [128, 512], F32)
            pE = AL("psum_tensor", "m_pE", [128, 512], F32)
            pF = AL("psum_tensor", "m_pF", [128, 512], F32)
            p96 = AL("psum_tensor", "m_p96", [96, 512], F32)

            SY(lambda s: s.dma_start(out=wdt_sb[:, :], in_=W_dt[:, :]))
            for ec in range(NE):
                SY(lambda s, ec=ec: s.dma_start(
                    out=m_xt[:, ec * 96:(ec + 1) * 96],
                    in_=W_x[ec * 128:(ec + 1) * 128, :]))
            wdt_ld = (sl.h, sl.n)
            wxc = VE(lambda v: v.tensor_copy(out=wx_bf[:, :],
                                             in_=m_xt[:, 0:NE * 96]),
                     waits=[wdt_ld])
            hcar_ms = GP(lambda g: g.memset(m_hcar[:, :], 0.0))
            ccar_ms = GP(lambda g: g.memset(m_ccar[:, :], 0.0))

            xn_marks = {}
            trm = {}
            xnTe_marks = {}
            winmm = {}
            stgevac = {}
            convlast = {}
            silu_marks = {}
            zevac = {}
            zst_marks = {}
            x96e = {}
            wxstop = {}
            bbe = {}
            cbe = {}
            relu_marks = {}
            deltaadd_marks = {}
            du_marks = {}
            xxq_marks = {}
            scan_marks = {}
            carry_marks = {}
            y_marks = {}
            ypr_marks = {}
            wdt_marks = {}
            dmm = {}
            dres = {}
            dstt = {}
            c1_marks = {}
            cmr_marks = {}
            y2_marks = {}
            bm3_marks = {}
            bq2_marks = {}

            def mg(d, k):
                v = d.get(k)
                return [v] if v is not None else []

            def tileA(c, jj):
                pb = 32 * (c % 2)
                cpar = c % 2
                ub = m_u[:, cpar * NE * 512:(cpar + 1) * NE * 512]
                x96 = m_x96[:, cpar * 512:(cpar + 1) * 512]
                j = 4 * c + jj
                xslot = m_xt[:, (jj % 2) * D:(jj % 2) * D + D]
                xm = SY(lambda s, j=j, xslot=xslot: s.dma_start(
                    out=xslot, in_=x0[j * 128:(j + 1) * 128, :]),
                    inc=r_mxl[jj % 2],
                    waits=mg(xn_marks, j - 2) + ([wxc] if j == 0 else []))
                cs = stat[:, pb + 8:pb + 9]
                cq = stat[:, pb + 9:pb + 10]
                cm_ = stat[:, pb + 10:pb + 11]
                cvv = stat[:, pb + 11:pb + 12]
                ctmp = stat[:, pb + 12:pb + 13]
                crq = stat[:, pb + 13:pb + 14]
                rs1 = stat[:, pb + 14:pb + 15]
                msq = stat[:, pb + 15:pb + 16]
                rs2 = stat[:, pb + 16:pb + 17]
                stok = stat[:, pb + 0 + jj:pb + 1 + jj]
                btok = stat[:, pb + 4 + jj:pb + 5 + jj]
                c1 = stat[:, pb + 24 + jj:pb + 25 + jj]
                cmr = stat[:, pb + 28 + jj:pb + 29 + jj]
                colwar = (mg(dstt, (c - 2, 1, 3)) + mg(dres, (c - 2, 1, 3)))
                s1 = SC(lambda s, xslot=xslot, cs=cs: s.activation(
                    out=m_xn[:, :], in_=xslot, func=AF.Identity,
                    accum_out=cs),
                    waits=[xm] + mg(trm, j - 1) + colwar)
                s2 = SC(lambda s, xslot=xslot, cq=cq: s.activation(
                    out=m_xn[:, :], in_=xslot, func=AF.Square,
                    accum_out=cq), waits=[s1])
                v1 = VE(lambda v, cs=cs, cm_=cm_: v.tensor_scalar(
                    out=cm_, in0=cs, scalar1=1.0 / D, scalar2=None,
                    op0=ALU.mult), waits=[s1] + colwar)
                v2 = VE(lambda v, cq=cq, cvv=cvv: v.tensor_scalar(
                    out=cvv, in0=cq, scalar1=1.0 / D, scalar2=None,
                    op0=ALU.mult), waits=[s2])
                v3 = VE(lambda v, cm_=cm_: v.tensor_tensor(
                    out=m_xn[:, 0:1], in0=cm_, in1=cm_, op=ALU.mult),
                    waits=[v1, s2])
                v4 = VE(lambda v, cvv=cvv: v.tensor_tensor(
                    out=cvv, in0=cvv, in1=m_xn[:, 0:1], op=ALU.subtract),
                    waits=[v2, v3])
                r1 = VE(lambda v, cvv=cvv, ctmp=ctmp: v.tensor_scalar(
                    out=ctmp, in0=cvv, scalar1=1e-5, scalar2=None,
                    op0=ALU.add), waits=[v4])
                r2 = VE(lambda v, ctmp=ctmp, crq=crq: v.reciprocal(
                    out=crq, in_=ctmp), waits=[r1])
                r3 = SC(lambda s, crq=crq, rs1=rs1: s.activation(
                    out=rs1, in_=crq, func=AF.Sqrt), waits=[r2])
                m1 = VE(lambda v, cvv=cvv, crq=crq, msq=msq: v.tensor_tensor(
                    out=msq, in0=cvv, in1=crq, op=ALU.mult), waits=[r2])
                m2 = VE(lambda v, msq=msq, ctmp=ctmp: v.tensor_scalar(
                    out=ctmp, in0=msq, scalar1=1e-6, scalar2=None,
                    op0=ALU.add), waits=[m1])
                m3 = VE(lambda v, ctmp=ctmp, crq=crq: v.reciprocal(
                    out=crq, in_=ctmp), waits=[m2, r3])
                m4 = SC(lambda s, crq=crq, rs2=rs2: s.activation(
                    out=rs2, in_=crq, func=AF.Sqrt), waits=[m3, r3])
                t1 = VE(lambda v, rs1=rs1, rs2=rs2, stok=stok: v.tensor_tensor(
                    out=stok, in0=rs1, in1=rs2, op=ALU.mult),
                    waits=[m4] + colwar)
                t2 = VE(lambda v, cm_=cm_, stok=stok, btok=btok:
                        v.tensor_tensor(out=btok, in0=cm_, in1=stok,
                                        op=ALU.mult), waits=[t1] + colwar)
                t3 = VE(lambda v, btok=btok: v.tensor_scalar(
                    out=btok, in0=btok, scalar1=-1.0, scalar2=None,
                    op0=ALU.mult), waits=[t2])
                c1_marks[(c, jj)] = VE(lambda v, rs1=rs1, c1=c1:
                                       v.tensor_scalar(
                    out=c1, in0=rs1, scalar1=1.0, scalar2=None, op0=ALU.add),
                    waits=[r3] + colwar)
                t5 = VE(lambda v, cm_=cm_, rs1=rs1, cmr=cmr: v.tensor_tensor(
                    out=cmr, in0=cm_, in1=rs1, op=ALU.mult),
                    waits=[r3] + colwar)
                cmr_marks[(c, jj)] = VE(lambda v, cmr=cmr: v.tensor_scalar(
                    out=cmr, in0=cmr, scalar1=-1.0, scalar2=None,
                    op0=ALU.mult), waits=[t5])
                xn_marks[j] = SC(lambda s, xslot=xslot, stok=stok, btok=btok:
                                 s.activation(out=m_xn[:, :], in_=xslot,
                                              func=AF.Identity, scale=stok,
                                              bias=btok),
                                 waits=[t3, v3, s2] + mg(trm, j - 1))
                for cc in range(8):
                    bank = pA if cc < 4 else pB
                    q = cc % 4
                    bankwar = mg(xnTe_marks, j - 1)
                    trm[j] = TE(lambda t, cc=cc, bank=bank, q=q: t.transpose(
                        out=bank[:, q * 128:(q + 1) * 128],
                        in_=m_xn[:, cc * 128:(cc + 1) * 128],
                        identity=ident_f[:, :]),
                        waits=[xn_marks[j]] + bankwar)
                for cc in range(8):
                    bank = pA if cc < 4 else pB
                    q = cc % 4
                    xnTe_marks[j] = SC(lambda s, cc=cc, bank=bank, q=q, jj=jj:
                                       s.activation(
                        out=m_xnT[:, cc * 512 + jj * 128:
                                  cc * 512 + (jj + 1) * 128],
                        in_=bank[:, q * 128:(q + 1) * 128], func=AF.Identity,
                        scale=norm_sb[:, cc:cc + 1]),
                        waits=[trm[j]]
                        + (mg(winmm, (c - 1) * 2 * NE + 2 * NE - 1)
                           if jj == 0 and cc == 0 else []))
            def etB(c, et):
                pb = 32 * (c % 2)
                cpar = c % 2
                ub = m_u[:, cpar * NE * 512:(cpar + 1) * NE * 512]
                x96 = m_x96[:, cpar * 512:(cpar + 1) * 512]
                ge = c * 2 * NE + et
                wslot = m_win[:, (et % 2) * 1024:(et % 2) * 1024 + 1024]
                wm = None
                Qw = SY if et % 2 == 0 else GP
                for cc in range(8):
                    wm = Qw(lambda s, et=et, cc=cc, wslot=wslot: s.dma_start(
                        out=wslot[:, cc * 128:(cc + 1) * 128],
                        in_=wbin_t[et, cc, :, :]), inc=r_win[et % 2], k=16,
                        waits=(mg(winmm, ge - 2) if cc == 0 else []))
                bank = pC if et % 2 == 0 else pD
                if et >= 2:
                    bwar = mg(stgevac, ge - 2) + mg(zevac, ge - 2)
                else:
                    bwar = mg(dres, (c - 2, 1, 2 + et))
                mmm = None
                for cc in range(8):
                    mmm = TE(lambda t, cc=cc, bank=bank, wslot=wslot: t.matmul(
                        bank[:, :], wslot[:, cc * 128:(cc + 1) * 128],
                        m_xnT[:, cc * 512:(cc + 1) * 512],
                        start=(cc == 0), stop=(cc == 7)),
                        inc=st if cc == 7 else None,
                        waits=([wm, xnTe_marks[4 * c + 3]] + bwar)
                        if cc == 0 else ())
                winmm[ge] = mmm
                if et < NE:
                    stg = m_stage[:, (et % 2) * 516:(et % 2) * 516 + 516]
                    sev = SC(lambda s, bank=bank, stg=stg: s.activation(
                        out=stg[:, 3:515], in_=bank[:, :], func=AF.Identity),
                        waits=[mmm] + mg(convlast, ge - 2))
                    stgevac[ge] = sev
                    cy = m_cv[:, (et % 2) * 512:(et % 2) * 512 + 512]
                    k0 = VE(lambda v, et=et, stg=stg: v.tensor_copy(
                        out=stg[:, 0:3], in_=m_ccar[:, et * 3:(et + 1) * 3]),
                        waits=[sev, (sg.h, ccar_ms[1])]
                        + mg(convlast, ge - 2))
                    k1 = VE(lambda v, et=et, stg=stg: v.tensor_copy(
                        out=m_ccar[:, et * 3:(et + 1) * 3],
                        in_=stg[:, 512:515]), waits=[k0])
                    k2 = VE(lambda v, et=et, stg=stg, cy=cy: v.tensor_scalar(
                        out=cy, in0=stg[:, 0:512],
                        scalar1=conv_sb[:, et * 4:et * 4 + 1], scalar2=None,
                        op0=ALU.mult),
                        waits=[k0] + mg(silu_marks, ge - 2))
                    for k in range(1, 4):
                        k2 = VE(lambda v, et=et, k=k, stg=stg, cy=cy:
                                v.scalar_tensor_tensor(
                            out=cy, in0=stg[:, k:512 + k],
                            scalar=conv_sb[:, et * 4 + k:et * 4 + k + 1],
                            in1=cy, op0=ALU.mult, op1=ALU.add), waits=[k2])
                    convlast[ge] = k2
                    silu_marks[ge] = SC(lambda s, et=et, cy=cy, ub=ub:
                                        s.activation(
                        out=ub[:, et * 512:(et + 1) * 512], in_=cy,
                        func=AF.Silu),
                        waits=[k2]
                        + (mg(wxstop, c - 2) + mg(du_marks, (c - 2) * NE + et)
                           + mg(y2_marks, (c - 2) * NE + et)
                           if c >= 2 else []))
                else:
                    ez = et - NE
                    gz = c * NE + ez
                    zslot = m_szs[:, (ez % 2) * 512:(ez % 2) * 512 + 512]
                    zev = SC(lambda s, bank=bank, zslot=zslot: s.activation(
                        out=zslot, in_=bank[:, :], func=AF.Silu),
                        waits=[mmm] + mg(zst_marks, gz - 2))
                    zevac[ge] = zev
                    zst_marks[gz] = SY(lambda s, ez=ez, c=c, zslot=zslot:
                                       s.dma_start(
                        out=z_hbm[ez * 128:(ez + 1) * 128,
                                  c * 512:(c + 1) * 512], in_=zslot),
                        inc=r_zst[ez % 2], k=16, waits=[zev])
            def wxAB(c):
                pb = 32 * (c % 2)
                cpar = c % 2
                ub = m_u[:, cpar * NE * 512:(cpar + 1) * NE * 512]
                x96 = m_x96[:, cpar * 512:(cpar + 1) * 512]
                wxm = None
                for ec in range(NE):
                    wxm = TE(lambda t, ec=ec, ub=ub: t.matmul(
                        p96[:, :], wx_bf[:, ec * 96:(ec + 1) * 96],
                        ub[:, ec * 512:(ec + 1) * 512],
                        start=(ec == 0), stop=(ec == NE - 1)),
                        inc=st if ec == NE - 1 else None,
                        waits=[silu_marks[c * 2 * NE + ec], wxc]
                        + (mg(x96e, c - 1) if ec == 0 else []))
                wxstop[c] = wxm
                x96e[c] = VE(lambda v, x96=x96: v.tensor_copy(
                    out=x96[:, :], in_=p96[0:96, :]),
                    waits=[wxm] + (mg(wdt_marks, (c - 2) * NE + NE - 1)
                                   if c >= 2 else [])
                    + ([(r_zbc[0].h, r_zbc[0].n)] if c >= 2 else []))
            def bcastC(c, n):
                pb = 32 * (c % 2)
                cpar = c % 2
                ub = m_u[:, cpar * NE * 512:(cpar + 1) * NE * 512]
                x96 = m_x96[:, cpar * 512:(cpar + 1) * 512]
                par = n % 2
                rsl = m_bcr[0:1, 0:1024]
                bsl = m_bcb[0:1, 0:1024]
                bm1 = GP(lambda g, n=n, rsl=rsl, x96=x96: g.dma_start(
                    out=rsl[0:1, 0:512], in_=x96[64 + n:65 + n, :]),
                    inc=r_zbc[0], k=16,
                    waits=[x96e[c]] + (mg(bm3_marks, (c, n - 1)) if n >= 1
                                       else mg(bm3_marks,
                                               (c - 1, NS - 1))))
                bm2 = GP(lambda g, n=n, rsl=rsl, x96=x96: g.dma_start(
                    out=rsl[0:1, 512:1024], in_=x96[80 + n:81 + n, :]),
                    inc=r_zbc[0], k=16)
                bm3 = VE(lambda v, rsl=rsl, bsl=bsl: v.tensor_copy(
                    out=bsl, in_=rsl),
                    waits=[(r_zbc[0].h, r_zbc[0].n)]
                    + (mg(bq2_marks, (c, n - 1)) if n >= 1 else
                       mg(bq2_marks, (c - 1, NS - 1))))
                bm3_marks[(c, n)] = bm3
                ebank = pE if par == 0 else pF
                if n >= 2:
                    ewar = mg(bbe, (c, n - 2)) + mg(cbe, (c, n - 2))
                else:
                    ewar = (mg(relu_marks, (c - 1) * NE + NE - 2 + n)
                            + mg(dres, (c - 1, 1, n)))
                bq1 = TE(lambda t, bsl=bsl, ebank=ebank: t.matmul(
                    ebank[:, :], ones_b[:, :], bsl[0:1, 0:512],
                    start=True, stop=True), waits=[bm3] + ewar)
                bbe[(c, n)] = SC(lambda s, n=n, ebank=ebank: s.activation(
                    out=m_Bb[:, n * 512:(n + 1) * 512], in_=ebank[:, :],
                    func=AF.Identity),
                    waits=[bq1]
                    + (mg(xxq_marks, (c - 1) * NE * 4 + NE * 4 - 1)
                       if n == 0 else []))
                bq2 = TE(lambda t, bsl=bsl, ebank=ebank: t.matmul(
                    ebank[:, :], ones_b[:, :], bsl[0:1, 512:1024],
                    start=True, stop=True), waits=[bbe[(c, n)]])
                bq2_marks[(c, n)] = bq2
                cbe[(c, n)] = VE(lambda v, n=n, ebank=ebank: v.tensor_copy(
                    out=bass.AP(m_Cb, n, [[NS * 512, 128], [NS, 512]]),
                    in_=ebank[:, :]),
                    waits=[bq2]
                    + (mg(y_marks, (c - 1) * NE + NE - 1)
                       if n == 0 else []))
            def etCpre(c, et):
                pb = 32 * (c % 2)
                cpar = c % 2
                ub = m_u[:, cpar * NE * 512:(cpar + 1) * NE * 512]
                x96 = m_x96[:, cpar * 512:(cpar + 1) * 512]
                te = c * NE + et
                par = et % 2
                dslot = m_delta[:, par * 512:(par + 1) * 512]
                spslot = m_sp[:, par * 512:(par + 1) * 512]
                duslot = m_du[:, par * 512:(par + 1) * 512]
                zslot = m_szl[:, (et % 4) * 512:(et % 4 + 1) * 512]
                bank = pE if par == 0 else pF
                zrm = SY(lambda s, et=et, c=c, zslot=zslot: s.dma_start(
                    out=zslot, in_=z_hbm[et * 128:(et + 1) * 128,
                                         c * 512:(c + 1) * 512]),
                    inc=r_zrl[et % 4],
                    waits=[zst_marks[te]] + mg(ypr_marks, te - 4))
                wdt_marks[te] = TE(lambda t, et=et, bank=bank, x96=x96:
                                   t.matmul(
                    bank[:, :], wdt_sb[:, et * 128:(et + 1) * 128],
                    x96[0:64, :], start=True, stop=True),
                    waits=[x96e[c], wdt_ld, bbe[(c, NS - 2 + par)],
                           cbe[(c, NS - 2 + par)]]
                    + mg(relu_marks, te - 2))
                a1 = SC(lambda s, et=et, bank=bank, spslot=spslot:
                        s.activation(
                    out=spslot, in_=bank[:, :], func=AF.Abs,
                    bias=bdt_sb[:, et:et + 1]),
                    waits=[wdt_marks[te]] + mg(deltaadd_marks, te - 2))
                a2 = SC(lambda s, spslot=spslot: s.activation(
                    out=spslot, in_=spslot, func=AF.Exp, scale=-1.0),
                    waits=[a1])
                a3 = SC(lambda s, spslot=spslot: s.activation(
                    out=spslot, in_=spslot, func=AF.Ln, bias=cbias[:, 2:3]),
                    waits=[a2])
                relu_marks[te] = SC(lambda s, et=et, bank=bank, dslot=dslot:
                                    s.activation(
                    out=dslot, in_=bank[:, :], func=AF.Relu,
                    bias=bdt_sb[:, et:et + 1]),
                    waits=[a3] + mg(du_marks, te - 2))
                deltaadd_marks[te] = VE(lambda v, dslot=dslot, spslot=spslot:
                                        v.tensor_tensor(
                    out=dslot, in0=dslot, in1=spslot, op=ALU.add),
                    waits=[relu_marks[te]])
                du_marks[te] = VE(lambda v, et=et, dslot=dslot, ub=ub,
                                  duslot=duslot: v.tensor_tensor(
                    out=duslot, in0=dslot,
                    in1=ub[:, et * 512:(et + 1) * 512], op=ALU.mult),
                    waits=[deltaadd_marks[te], silu_marks[c * 2 * NE + et]]
                    + (mg(xxq_marks, (te - 2) * 4 + 3)))

            def etCscan(c, et):
                pb = 32 * (c % 2)
                cpar = c % 2
                ub = m_u[:, cpar * NE * 512:(cpar + 1) * NE * 512]
                x96 = m_x96[:, cpar * 512:(cpar + 1) * 512]
                te = c * NE + et
                par = et % 2
                dslot = m_delta[:, par * 512:(par + 1) * 512]
                spslot = m_sp[:, par * 512:(par + 1) * 512]
                duslot = m_du[:, par * 512:(par + 1) * 512]
                zslot = m_szl[:, (et % 4) * 512:(et % 4 + 1) * 512]
                bank = pE if par == 0 else pF
                for q in range(4):
                    gq = te * 4 + q
                    xpar = gq % 2
                    xslot = m_xx[:, xpar * 2048:(xpar + 1) * 2048]
                    if gq >= 2:
                        pg = gq - 2
                        war = mg(scan_marks,
                                 (pg // 4) * 16 + (pg % 4) * 4 + 3)
                    else:
                        war = []
                    xxq_marks[gq] = GP(lambda g, q=q, et=et, xslot=xslot:
                                       g.tensor_tensor(
                        out=xslot,
                        in0=bass.AP(m_du, (et % 2) * 512,
                                    [[1024, 128], [0, 4], [1, 512]]),
                        in1=m_Bb[:, q * 4 * 512:(q + 1) * 4 * 512],
                        op=ALU.mult),
                        waits=[du_marks[te], bbe[(c, q * 4 + 3)]] + war)
                    for n in range(q * 4, q * 4 + 4):
                        gn = te * 16 + n
                        npar = n % 4
                        daslot = m_dA[:, npar * 512:(npar + 1) * 512]
                        em = SC(lambda s, et=et, n=n, dslot=dslot,
                                daslot=daslot: s.activation(
                            out=daslot, in_=dslot, func=AF.Exp,
                            scale=A_sb[:, et * NS + n:et * NS + n + 1]),
                            waits=[deltaadd_marks[te], relu_marks[te]]
                            + mg(scan_marks, gn - 4))
                        xs_ap = m_xx[:, xpar * 2048 + (n % 4) * 512:
                                     xpar * 2048 + (n % 4 + 1) * 512]
                        scan_marks[gn] = VE(
                            lambda v, et=et, n=n, daslot=daslot, xs_ap=xs_ap:
                            v.tensor_tensor_scan(
                                out=bass.AP(m_h, n,
                                            [[NS * 512, 128], [NS, 512]]),
                                data0=daslot, data1=xs_ap,
                                initial=m_hcar[:, et * NS + n:
                                               et * NS + n + 1],
                                op0=ALU.mult, op1=ALU.add),
                            waits=[em, xxq_marks[gq],
                                   (sg.h, hcar_ms[1])]
                            + (mg(y_marks, te - 1) if n == 0 else [])
                            + (mg(carry_marks, te - NE) if n == 0 else []))
                carry_marks[te] = VE(lambda v, et=et: v.tensor_copy(
                    out=m_hcar[:, et * NS:(et + 1) * NS],
                    in_=m_h[:, 511 * NS:512 * NS]),
                    waits=[scan_marks[te * 16 + NS - 1]])
                ym1 = VE(lambda v: v.tensor_tensor(
                    out=m_h[:, :], in0=m_h[:, :], in1=m_Cb[:, :],
                    op=ALU.mult),
                    waits=[carry_marks[te], cbe[(c, NS - 1)]])
                ym2 = VE(lambda v: v.tensor_reduce(
                    out=m_y[:, 0:512],
                    in_=bass.AP(m_h, 0, [[NS * 512, 128], [NS, 512], [1, NS]]),
                    axis=AX.X, op=ALU.add),
                    waits=[ym1] + (mg(dstt, (c - 1, 1, 3))
                                   if et == 0 else []))
                y_marks[te] = ym2
                ym3 = VE(lambda v, et=et, ub=ub: v.scalar_tensor_tensor(
                    out=m_y[:, 512:1024],
                    in0=ub[:, et * 512:(et + 1) * 512],
                    scalar=D_sb[:, et:et + 1], in1=m_y[:, 0:512],
                    op0=ALU.mult, op1=ALU.add),
                    waits=[ym2, silu_marks[c * 2 * NE + et]])
                y2_marks[te] = ym3
                ypr_marks[te] = VE(lambda v, et=et, zslot=zslot:
                                   v.tensor_tensor(
                    out=m_ypr[:, et * 512:(et + 1) * 512],
                    in0=m_y[:, 512:1024], in1=zslot, op=ALU.mult),
                    waits=[ym3, (r_zrl[et % 4].h, r_zrl[et % 4].n)]
                    + (mg(dmm, (c - 1, 1, NE - 1, 3)) if et < 2 else []))
            def stageD(c):
                pb = 32 * (c % 2)
                cpar = c % 2
                ub = m_u[:, cpar * NE * 512:(cpar + 1) * NE * 512]
                x96 = m_x96[:, cpar * 512:(cpar + 1) * 512]
                for half in range(2):
                    for et in range(NE):
                        wslot = m_wo[:, (et % 2) * 512:(et % 2) * 512 + 512]
                        wom = SY(lambda s, half=half, et=et, wslot=wslot:
                                 s.dma_start(
                            out=wslot, in_=wbout_t[half, et, :, :]),
                            inc=r_wou[et % 2],
                            waits=mg(dmm, (c, half, et - 2, 3))
                            + mg(dmm, (c - 1, 1, NE + et - 2 if False else
                                       NE - 2 + et, 3)))
                        for jj in range(4):
                            bank = [pE, pF, pC, pD][jj]
                            if et == 0:
                                if half == 0:
                                    if jj < 2:
                                        bwar = [relu_marks[c * NE + NE - 2 + jj]]
                                    else:
                                        bwar = (mg(zevac, (c + 1) * 2 * NE
                                                   + 2 * NE - 4 + jj)
                                                + mg(zevac, c * 2 * NE
                                                     + 2 * NE - 4 + jj))
                                else:
                                    bwar = mg(dres, (c, 0, jj))
                            else:
                                bwar = []
                            dmm[(c, half, et, jj)] = TE(
                                lambda t, et=et, jj=jj, bank=bank, wslot=wslot:
                                t.matmul(
                                    bank[:, :],
                                    m_ypr[:, et * 512 + jj * 128:
                                          et * 512 + (jj + 1) * 128],
                                    wslot, start=(et == 0), stop=(et == NE - 1)),
                                inc=st if jj == 3 else None,
                                waits=([wom, ypr_marks[c * NE + et]] + bwar)
                                if (jj == 0 or et == 0) else ())
                    for jj in range(4):
                        j = 4 * c + jj
                        bank = [pE, pF, pC, pD][jj]
                        xrm = SY(lambda s, j=j, half=half: s.dma_start(
                            out=m_xr[:, :], in_=x0[j * 128:(j + 1) * 128,
                                                   half * 512:(half + 1) * 512]),
                            inc=r_mxr[0],
                            waits=mg(dstt, (c, half, jj - 1))
                            + mg(dstt, (c, half - 1, 3))
                            + mg(dstt, (c - 1, 1, 3)))
                        dres[(c, half, jj)] = SC(
                            lambda s, jj=jj, bank=bank, pb=pb: s.activation(
                                out=m_y[:, (jj % 2) * 512:(jj % 2) * 512 + 512],
                                in_=bank[:, :], func=AF.Identity,
                                bias=stat[:, pb + 28 + jj:pb + 29 + jj]),
                            waits=[dmm[(c, half, NE - 1, 3)],
                                   ypr_marks[c * NE + NE - 1],
                                   cmr_marks[(c, jj)]]
                            + mg(dstt, (c, half, jj - 2))
                            + (mg(dstt, (c, half - 1, 2 + jj)) if jj < 2 else []))
                        dstt[(c, half, jj)] = VE(
                            lambda v, jj=jj, pb=pb: v.scalar_tensor_tensor(
                                out=m_xo[:, :], in0=m_xr[:, :],
                                scalar=stat[:, pb + 24 + jj:pb + 25 + jj],
                                in1=m_y[:, (jj % 2) * 512:(jj % 2) * 512 + 512],
                                op0=ALU.mult, op1=ALU.add),
                            waits=[xrm, dres[(c, half, jj)],
                                   c1_marks[(c, jj)], (s_dst.h, s_dst.n)])
                        SY(lambda s, j=j, half=half: s.dma_start(
                            out=xs1[j * 128:(j + 1) * 128,
                                    half * 512:(half + 1) * 512], in_=m_xo[:, :]),
                            inc=s_dst, k=16, waits=[dstt[(c, half, jj)]])

            for jj in range(4):
                tileA(0, jj)
            for et in range(2 * NE):
                etB(0, et)
            wxAB(0)
            for c in range(NCH):
                for n in range(NS):
                    bcastC(c, n)
                etCpre(c, 0)
                etCpre(c, 1)
                etCscan(c, 0)
                if c + 1 < NCH:
                    for jj in range(4):
                        tileA(c + 1, jj)
                for et in range(1, 7):
                    etCpre(c, et + 1)
                    etCscan(c, et)
                for et in range(7, NE - 1):
                    if c + 1 < NCH:
                        for eb in range(4 * (et - 7), 4 * (et - 7) + 4):
                            etB(c + 1, eb)
                    etCpre(c, et + 1)
                    etCscan(c, et)
                if c + 1 < NCH:
                    wxAB(c + 1)
                etCscan(c, NE - 1)
                stageD(c)
            pm.run()

    def phase_ffn(xin_ap, xout_ap, tag):
        pf = new_prog()
        with ExitStack() as esp:
            AL = lambda k, n, s, d: esp.enter_context(getattr(nc, k)(n, list(s), d))
            f_wg = AL("sbuf_tensor", tag + "_wg", [128, 8 * INNER], F8)
            f_wu = AL("sbuf_tensor", tag + "_wu", [128, 8 * INNER], F8)
            f_wd = AL("sbuf_tensor", tag + "_wd", [128, 64 * 512], F8)
            f_xt = AL("sbuf_tensor", tag + "_xt", [128, 2 * D], F32)
            f_xln = AL("sbuf_tensor", tag + "_xln", [128, D], F32)
            f_xqi = AL("sbuf_tensor", tag + "_xqi", [128, D], F32)
            f_xqT = AL("sbuf_tensor", tag + "_xqT", [128, 2 * D], BF16)
            f_g = AL("sbuf_tensor", tag + "_g", [128, 2 * INNER], F32)
            f_hqi = AL("sbuf_tensor", tag + "_hqi", [128, INNER], F32)
            f_hqT = AL("sbuf_tensor", tag + "_hqT", [128, INNER], BF16)
            f_xo = AL("sbuf_tensor", tag + "_xo", [128, D], F32)
            fpA = AL("psum_tensor", tag + "_pA", [128, 512], F32)
            fpB = AL("psum_tensor", tag + "_pB", [128, 512], F32)
            fpC = AL("psum_tensor", tag + "_pC", [128, 512], F32)
            fpD = AL("psum_tensor", tag + "_pD", [128, 512], F32)
            fpX = AL("psum_tensor", tag + "_pX", [128, 512], F32)
            fpY = AL("psum_tensor", tag + "_pY", [128, 512], F32)

            for cc in range(8):
                for ob in range(8):
                    SY(lambda s, cc=cc, ob=ob: s.dma_start(
                        out=f_wg[:, (cc * 8 + ob) * 512:(cc * 8 + ob + 1) * 512],
                        in_=wg8[cc, ob, :, :]))
                    SY(lambda s, cc=cc, ob=ob: s.dma_start(
                        out=f_wu[:, (cc * 8 + ob) * 512:(cc * 8 + ob + 1) * 512],
                        in_=wu8[cc, ob, :, :]))
            w_ready = (sl.h, sl.n)
            for half in range(2):
                for oc in range(32):
                    SD(lambda s, oc=oc, half=half: s.dma_start(
                        out=f_wd[:, (half * 32 + oc) * 512:
                                 (half * 32 + oc + 1) * 512],
                        in_=wd8[oc, half, :, :]))
            wd_ready = (sl2.h, sl2.n)

            xm_m = {}
            xln_m = {}
            amax_m = {}
            sx_m = {}
            ivg_m = {}
            ivu_m = {}
            ivd_m = {}
            xqi_m = {}
            xqtr_m = {}
            xqte_m = {}
            gmm = {}
            umm = {}
            gev = {}
            hstt = {}
            amaxh_m = {}
            hqi_m = {}
            hqtr_m = {}
            hqte_m = {}
            dmmf = {}
            dstt_m = {}
            st_m = {}

            def mg(d, k):
                v = d.get(k)
                return [v] if v is not None else []

            def front(j):
                pb = 24 * (j % 2)
                xslot = f_xt[:, (j % 2) * D:(j % 2) * D + D]
                xqTs = f_xqT[:, (j % 2) * D:(j % 2) * D + D]
                xm = SY(lambda s, j=j, xslot=xslot: s.dma_start(
                    out=xslot, in_=xin_ap[j * 128:(j + 1) * 128, :]),
                    inc=r_fxl[j % 2], waits=mg(dstt_m, (j - 2, 1)))
                xm_m[j] = xm
                cs = stat[:, pb + 0:pb + 1]
                cq = stat[:, pb + 1:pb + 2]
                cm_ = stat[:, pb + 2:pb + 3]
                cvv = stat[:, pb + 3:pb + 4]
                ctmp = stat[:, pb + 4:pb + 5]
                crq = stat[:, pb + 5:pb + 6]
                rs_ = stat[:, pb + 6:pb + 7]
                nb_ = stat[:, pb + 7:pb + 8]
                amx = stat[:, pb + 8:pb + 9]
                sx = stat[:, pb + 9:pb + 10]
                ivg = stat[:, pb + 10:pb + 11]
                ivu = stat[:, pb + 11:pb + 12]
                cmx = stat[:, pb + 15:pb + 16]
                cmn = stat[:, pb + 16:pb + 17]
                qsc = stat[:, pb + 17:pb + 18]
                qbi = stat[:, pb + 18:pb + 19]
                colwar = mg(dstt_m, (j - 2, 1)) + mg(ivd_m, j - 2)
                s1 = SC(lambda s, xslot=xslot, cs=cs: s.activation(
                    out=f_xln[:, :], in_=xslot, func=AF.Identity, accum_out=cs),
                    waits=[xm] + mg(amax_m, j - 1) + mg(xqi_m, j - 1) + colwar)
                s2 = SC(lambda s, xslot=xslot, cq=cq: s.activation(
                    out=f_xln[:, :], in_=xslot, func=AF.Square, accum_out=cq),
                    waits=[s1])
                w1 = VE(lambda v, xslot=xslot, cmx=cmx: v.tensor_reduce(
                    out=cmx, in_=xslot, axis=AX.X, op=ALU.max),
                    waits=[xm] + colwar)
                w2 = VE(lambda v, xslot=xslot, cmn=cmn: v.tensor_reduce(
                    out=cmn, in_=xslot, axis=AX.X, op=ALU.min), waits=[w1])
                v1 = VE(lambda v, cs=cs, cm_=cm_: v.tensor_scalar(
                    out=cm_, in0=cs, scalar1=1.0 / D, scalar2=None,
                    op0=ALU.mult), waits=[s1] + colwar)
                v2 = VE(lambda v, cq=cq, cvv=cvv: v.tensor_scalar(
                    out=cvv, in0=cq, scalar1=1.0 / D, scalar2=None,
                    op0=ALU.mult), waits=[s2])
                v3 = VE(lambda v, cm_=cm_: v.tensor_tensor(
                    out=f_xln[:, 0:1], in0=cm_, in1=cm_, op=ALU.mult),
                    waits=[v1, s2])
                v4 = VE(lambda v, cvv=cvv: v.tensor_tensor(
                    out=cvv, in0=cvv, in1=f_xln[:, 0:1], op=ALU.subtract),
                    waits=[v2, v3])
                r1 = VE(lambda v, cvv=cvv, ctmp=ctmp: v.tensor_scalar(
                    out=ctmp, in0=cvv, scalar1=1e-5, scalar2=None, op0=ALU.add),
                    waits=[v4])
                r2 = VE(lambda v, ctmp=ctmp, crq=crq: v.reciprocal(
                    out=crq, in_=ctmp), waits=[r1])
                r3 = SC(lambda s, crq=crq, rs_=rs_: s.activation(
                    out=rs_, in_=crq, func=AF.Sqrt), waits=[r2])
                g1 = VE(lambda v, cmx=cmx, cm_=cm_: v.tensor_tensor(
                    out=cmx, in0=cmx, in1=cm_, op=ALU.subtract),
                    waits=[w2, v1])
                g2 = VE(lambda v, cm_=cm_, cmn=cmn: v.tensor_tensor(
                    out=cmn, in0=cm_, in1=cmn, op=ALU.subtract), waits=[g1])
                g3 = VE(lambda v, cmx=cmx, cmn=cmn: v.tensor_tensor(
                    out=cmx, in0=cmx, in1=cmn, op=ALU.max), waits=[g2])
                g4 = VE(lambda v, cmx=cmx, rs_=rs_, amx=amx: v.tensor_tensor(
                    out=amx, in0=cmx, in1=rs_, op=ALU.mult),
                    waits=[g3, r3] + colwar)
                a2 = VE(lambda v, amx=amx: v.tensor_scalar(
                    out=amx, in0=amx, scalar1=1e-5, scalar2=1.0 / 127.0,
                    op0=ALU.max, op1=ALU.mult), waits=[g4])
                sx_m[j] = VE(lambda v, amx=amx, sx=sx: v.reciprocal(
                    out=sx, in_=amx), waits=[a2])
                ivg_m[j] = VE(lambda v, amx=amx, ivg=ivg: v.tensor_tensor(
                    out=ivg, in0=amx, in1=rswg[:, 0:1], op=ALU.mult),
                    waits=[sx_m[j]])
                ivu_m[j] = VE(lambda v, amx=amx, ivu=ivu: v.tensor_tensor(
                    out=ivu, in0=amx, in1=rswg[:, 1:2], op=ALU.mult),
                    waits=[ivg_m[j]])
                q1 = VE(lambda v, rs_=rs_, sx=sx, qsc=qsc: v.tensor_tensor(
                    out=qsc, in0=rs_, in1=sx, op=ALU.mult),
                    waits=[sx_m[j]] + colwar)
                q2 = VE(lambda v, cm_=cm_, qsc=qsc, qbi=qbi: v.tensor_tensor(
                    out=qbi, in0=cm_, in1=qsc, op=ALU.mult), waits=[q1])
                q3 = VE(lambda v, qbi=qbi: v.tensor_scalar(
                    out=qbi, in0=qbi, scalar1=-1.0, scalar2=12582912.0,
                    op0=ALU.mult, op1=ALU.add), waits=[q2])
                amax_m[j] = q3
                xln_m[j] = q3
                xqi_m[j] = SC(lambda s, xslot=xslot, qsc=qsc, qbi=qbi:
                              s.activation(
                    out=f_xqi[:, :], in_=xslot, func=AF.Identity, scale=qsc,
                    bias=qbi), waits=[q3, v3] + mg(xqtr_m, j - 1))
                for rnd in range(2):
                    bank = fpA if rnd == 0 else fpB
                    for cc4 in range(4):
                        cc = rnd * 4 + cc4
                        xqtr_m[j] = TE(lambda t, cc=cc, cc4=cc4, bank=bank:
                                       t.transpose(
                            out=bank[:, cc4 * 128:(cc4 + 1) * 128],
                            in_=f_xqi[:, cc * 128:(cc + 1) * 128],
                            identity=ident_f[:, :]),
                            waits=([xqi_m[j]]
                                   + mg(gev, (j - 1, 6 + rnd)))
                            if cc4 == 0 else [xqi_m[j]])
                    xqte_m[(j, rnd)] = SC(lambda s, rnd=rnd, xqTs=xqTs:
                                          s.activation(
                        out=xqTs[:, rnd * 512:(rnd + 1) * 512],
                        in_=(fpA if rnd == 0 else fpB)[:, 0:512],
                        func=AF.Identity, bias=cbias[:, 1:2]),
                        waits=[xqtr_m[j]]
                        + (mg(gmm, (j - 1, 6 + rnd))
                           + mg(umm, (j - 1, 6 + rnd))))

            def mid(j):
                pb = 24 * (j % 2)
                xqTs = f_xqT[:, (j % 2) * D:(j % 2) * D + D]
                gs = f_g[:, (j % 2) * INNER:(j % 2) * INNER + INNER]
                ivg = stat[:, pb + 10:pb + 11]
                ivu = stat[:, pb + 11:pb + 12]
                amh = stat[:, pb + 12:pb + 13]
                sh = stat[:, pb + 13:pb + 14]
                ivd = stat[:, pb + 14:pb + 15]
                for ob in range(8):
                    bg = fpA if ob % 2 == 0 else fpB
                    bu = fpC if ob % 2 == 0 else fpD
                    gwar = (mg(gev, (j, ob - 2)) if ob >= 2
                            else [xqte_m[(j, ob)]])
                    uwar = (mg(hstt, (j, ob - 2)) if ob >= 2
                            else mg(hqtr_m, (j - 1, 2 + 2 * ob)))
                    g0 = None
                    u0 = None
                    for cc in range(8):
                        g0 = TE(lambda t, cc=cc, ob=ob, bg=bg, xqTs=xqTs:
                                t.matmul(
                            bg[:, :], xqTs[:, cc * 128:(cc + 1) * 128],
                            f_wg[:, (cc * 8 + ob) * 512:(cc * 8 + ob + 1) * 512],
                            start=(cc == 0), stop=(cc == 7)),
                            inc=st if cc == 7 else None,
                            waits=([xqte_m[(j, 1)], w_ready] + gwar)
                            if cc == 0 else ())
                        u0 = TE(lambda t, cc=cc, ob=ob, bu=bu, xqTs=xqTs:
                                t.matmul(
                            bu[:, :], xqTs[:, cc * 128:(cc + 1) * 128],
                            f_wu[:, (cc * 8 + ob) * 512:(cc * 8 + ob + 1) * 512],
                            start=(cc == 0), stop=(cc == 7)),
                            inc=st if cc == 7 else None,
                            waits=(uwar if cc == 0 else ()))
                    gmm[(j, ob)] = g0
                    umm[(j, ob)] = u0
                    gev[(j, ob)] = SC(lambda s, ob=ob, bg=bg, ivg=ivg, gs=gs:
                                      s.activation(
                        out=gs[:, ob * 512:(ob + 1) * 512], in_=bg[:, :],
                        func=AF.Silu, scale=ivg),
                        waits=[g0, ivg_m[j]] + mg(amaxh_m, j - 2)
                        + mg(hqi_m, j - 2))
                    hstt[(j, ob)] = VE(lambda v, ob=ob, bu=bu, ivu=ivu, gs=gs:
                                       v.scalar_tensor_tensor(
                        out=gs[:, ob * 512:(ob + 1) * 512], in0=bu[:, :],
                        scalar=ivu, in1=gs[:, ob * 512:(ob + 1) * 512],
                        op0=ALU.mult, op1=ALU.mult),
                        waits=[u0, gev[(j, ob)], ivu_m[j]])
                colwar = mg(dstt_m, (j - 2, 1)) + mg(ivd_m, j - 2)
                amaxh_m[j] = VE(lambda v, amh=amh, gs=gs: v.tensor_reduce(
                    out=amh, in_=gs[:, :], axis=AX.X, op=ALU.max,
                    apply_absolute_value=True), waits=[hstt[(j, 7)]] + colwar)
                h2 = VE(lambda v, amh=amh: v.tensor_scalar(
                    out=amh, in0=amh, scalar1=1e-5, scalar2=1.0 / 127.0,
                    op0=ALU.max, op1=ALU.mult), waits=[amaxh_m[j]])
                h3 = VE(lambda v, amh=amh, sh=sh: v.reciprocal(
                    out=sh, in_=amh), waits=[h2])
                ivd_m[j] = VE(lambda v, amh=amh, ivd=ivd: v.tensor_tensor(
                    out=ivd, in0=amh, in1=rswg[:, 2:3], op=ALU.mult),
                    waits=[h3])
                hqi_m[j] = SC(lambda s, sh=sh, gs=gs: s.activation(
                    out=f_hqi[:, :], in_=gs[:, :], func=AF.Identity, scale=sh,
                    bias=cbias[:, 0:1]),
                    waits=[h3, hstt[(j, 7)]] + mg(hqtr_m, (j - 1, 7)))

            def tail(j):
                pb = 24 * (j % 2)
                xslot = f_xt[:, (j % 2) * D:(j % 2) * D + D]
                ivd = stat[:, pb + 14:pb + 15]
                for grp in range(8):
                    bank = fpC if grp % 2 == 0 else fpD
                    for q in range(4):
                        oc = grp * 4 + q
                        hqtr_m[(j, grp)] = TE(lambda t, oc=oc, q=q, bank=bank:
                                              t.transpose(
                            out=bank[:, q * 128:(q + 1) * 128],
                            in_=f_hqi[:, oc * 128:(oc + 1) * 128],
                            identity=ident_f[:, :]),
                            waits=([hqi_m[j]] + mg(hqte_m, (j, grp - 2))
                                   + ([umm[(j, 6 + grp)]] if grp < 2 else []))
                            if q == 0 else [hqi_m[j]])
                    hqte_m[(j, grp)] = SC(lambda s, grp=grp, bank=bank:
                                          s.activation(
                        out=f_hqT[:, grp * 512:(grp + 1) * 512],
                        in_=bank[:, 0:512], func=AF.Identity,
                        bias=cbias[:, 1:2]),
                        waits=[hqtr_m[(j, grp)]] + mg(dmmf, (j - 1, 1, 3)))
                for half in range(2):
                    bank = fpX if half == 0 else fpY
                    for grp in range(4):
                        for q in range(8):
                            oc = grp * 8 + q
                            wslot = f_wd[:, (half * 32 + oc) * 512:
                                         (half * 32 + oc + 1) * 512]
                            dmmf[(j, half, grp)] = TE(
                                lambda t, oc=oc, bank=bank, wslot=wslot:
                                t.matmul(
                                    bank[:, :],
                                    f_hqT[:, oc * 128:(oc + 1) * 128], wslot,
                                    start=(oc == 0), stop=(oc == 31)),
                                inc=st if oc % 8 == 7 else None,
                                waits=([wd_ready, hqte_m[(j, grp * 2 + 1)]]
                                       + (mg(dstt_m, (j - 1, half))
                                          if grp == 0 else []))
                                if q == 0 else ())
                    dstt_m[(j, half)] = VE(
                        lambda v, half=half, bank=bank, xslot=xslot, ivd=ivd:
                        v.scalar_tensor_tensor(
                            out=f_xo[:, half * 512:(half + 1) * 512],
                            in0=bank[:, :], scalar=ivd,
                            in1=xslot[:, half * 512:(half + 1) * 512],
                            op0=ALU.mult, op1=ALU.add),
                        waits=[dmmf[(j, half, 3)], ivd_m[j], xm_m[j],
                               (sw.h, sw.n)])
                st_m[j] = GP(lambda g, j=j: g.dma_start(
                    out=xout_ap[j * 128:(j + 1) * 128, :], in_=f_xo[:, :]),
                    inc=sw, k=16, waits=[dstt_m[(j, 0)], dstt_m[(j, 1)]])

            for j in range(NT):
                front(j)
                if j >= 1:
                    tail(j - 1)
                mid(j)
            tail(NT - 1)
        pf.run()

    if "F1" in phases:
        phase_ffn(xs1, xs2, "f1")

    if "A" in phases:
        pa = new_prog()
        scale2 = 1.0 / (float(T) * float(T))
        with ExitStack() as esp:
            AL = lambda k, n, s, d: esp.enter_context(getattr(nc, k)(n, list(s), d))
            a_w = AL("sbuf_tensor", "a_w", [128, 5 * 2 * 8 * 512], BF16)
            a_lngb = AL("sbuf_tensor", "a_lngb", [128, D], F32)
            a_lngr = AL("sbuf_tensor", "a_lngr", [1, D], F32)
            a_xt = AL("sbuf_tensor", "a_xt", [128, 2 * D], F32)
            a_xln = AL("sbuf_tensor", "a_xln", [128, D], F32)
            a_xlnT = AL("sbuf_tensor", "a_xlnT", [128, D], BF16)
            a_f = AL("sbuf_tensor", "a_f", [128, 2 * D], F32)
            a_sq = AL("sbuf_tensor", "a_sq", [128, D], F32)
            a_nrm = AL("sbuf_tensor", "a_nrm", [128, 2 * H], F32)
            a_qn = AL("sbuf_tensor", "a_qn", [128, D], F32)
            a_kn = AL("sbuf_tensor", "a_kn", [128, D], BF16)
            a_vn = AL("sbuf_tensor", "a_vn", [128, D], BF16)
            a_st = AL("sbuf_tensor", "a_st", [128, 2 * 1024], F32)
            a_S = AL("sbuf_tensor", "a_S", [128, 8 * 64], BF16)
            a_qT = AL("sbuf_tensor", "a_qT", [128, D], BF16)
            a_og = AL("sbuf_tensor", "a_og", [128, D], F32)
            a_ogT = AL("sbuf_tensor", "a_ogT", [128, D], BF16)
            a_xo = AL("sbuf_tensor", "a_xo", [128, D], F32)
            apA = AL("psum_tensor", "a_pA", [128, 512], F32)
            apB = AL("psum_tensor", "a_pB", [128, 512], F32)
            apC = AL("psum_tensor", "a_pC", [128, 512], F32)
            apD = AL("psum_tensor", "a_pD", [128, 512], F32)
            apT = AL("psum_tensor", "a_pT", [128, 1024], F32)

            def mg(d, k):
                v = d.get(k)
                return [v] if v is not None else []

            for pi in range(5):
                for half in range(2):
                    for cc in range(8):
                        off = ((pi * 2 + half) * 8 + cc) * 512
                        SY(lambda s, pi=pi, half=half, cc=cc, off=off:
                           s.dma_start(out=a_w[:, off:off + 512],
                                       in_=wat_t[pi, half, cc, :, :]))
            aw_ready = (sl.h, sl.n)
            SY(lambda s: s.dma_start(out=a_lngr[:, :], in_=ln_g[:][None, :]),
               inc=s_lng)
            lgm = {}
            for half in range(2):
                bank = apA if half == 0 else apB
                q1 = TE(lambda t, half=half, bank=bank: t.matmul(
                    bank[:, :], ones_f[:, :],
                    a_lngr[0:1, half * 512:(half + 1) * 512],
                    start=True, stop=True), waits=[(s_lng.h, s_lng.n)])
                lgm[half] = VE(lambda v, half=half, bank=bank: v.tensor_copy(
                    out=a_lngb[:, half * 512:(half + 1) * 512], in_=bank[:, :]),
                    waits=[q1])
            stm = GP(lambda g: g.memset(a_st[:, :], 0.0))

            xln_m = {}
            xltr_m = {}
            xlte_m = {}
            pmm = {}
            pev = {}
            fm_m = {}
            qst_m = {}
            kn_m = {}
            vn_m = {}
            smm = {}
            sadd = {}

            def proj_mm(j, pi, half, bank, bwar):
                for cc in range(8):
                    off = ((pi * 2 + half) * 8 + cc) * 512
                    TE(lambda t, cc=cc, off=off, bank=bank: t.matmul(
                        bank[:, :], a_xlnT[:, cc * 128:(cc + 1) * 128],
                        a_w[:, off:off + 512], start=(cc == 0), stop=False),
                        inc=None,
                        waits=([xlte_m[j], aw_ready] + bwar) if cc == 0 else ())
                return TE(lambda t, pi=pi, half=half, bank=bank: t.matmul(
                    bank[:, :], ones_b[:, :],
                    battn[0:1, pi * D + half * 512:pi * D + (half + 1) * 512],
                    start=False, stop=True))

            def fmap_norm(j, dst_bf, nrmoff, evm, extra_dst_war, sl_=0):
                fsl = a_f[:, sl_ * D:(sl_ + 1) * D]
                f1 = VE(lambda v, fsl=fsl: v.tensor_scalar(
                    out=fsl, in0=fsl, scalar1=0.5, scalar2=None,
                    op0=ALU.add), waits=[evm])
                f2 = VE(lambda v, fsl=fsl: v.tensor_tensor(
                    out=a_sq[:, :], in0=fsl, in1=fsl, op=ALU.mult),
                    waits=[f1])
                nr = a_nrm[:, nrmoff:nrmoff + H]
                f3 = VE(lambda v, nr=nr: v.tensor_reduce(
                    out=nr, in_=bass.AP(a_sq, 0, [[D, 128], [DH, H], [1, DH]]),
                    axis=AX.X, op=ALU.add), waits=[f2])
                f4 = SC(lambda s, nr=nr: s.activation(out=nr, in_=nr,
                                                      func=AF.Sqrt), waits=[f3])
                f5 = VE(lambda v, nr=nr: v.tensor_scalar(
                    out=nr, in0=nr, scalar1=1e-6, scalar2=None, op0=ALU.add),
                    waits=[f4])
                f6 = VE(lambda v, nr=nr: v.reciprocal(out=nr, in_=nr),
                        waits=[f5])
                last = GP(lambda g, fsl=fsl, nrmoff=nrmoff: g.tensor_tensor(
                    out=dst_bf[:, 0:D], in0=fsl,
                    in1=bass.AP(a_nrm, nrmoff, [[2 * H, 128], [1, H], [0, DH]]),
                    op=ALU.mult), waits=[f6] + extra_dst_war)
                return last

            for j in range(NT):
                pb = 24 * (j % 2)
                xslot = a_xt[:, (j % 2) * D:(j % 2) * D + D]
                xm = SY(lambda s, j=j, xslot=xslot: s.dma_start(
                    out=xslot, in_=xs2[j * 128:(j + 1) * 128, :]),
                    inc=r_axl[j % 2], waits=mg(xln_m, j - 2))
                cs = stat[:, pb + 0:pb + 1]
                cq = stat[:, pb + 1:pb + 2]
                cm_ = stat[:, pb + 2:pb + 3]
                cvv = stat[:, pb + 3:pb + 4]
                ctmp = stat[:, pb + 4:pb + 5]
                crq = stat[:, pb + 5:pb + 6]
                rs_ = stat[:, pb + 6:pb + 7]
                nb_ = stat[:, pb + 7:pb + 8]
                colwar = mg(xln_m, j - 2)
                s1 = SC(lambda s, xslot=xslot, cs=cs: s.activation(
                    out=a_xln[:, :], in_=xslot, func=AF.Identity, accum_out=cs),
                    waits=[xm] + mg(xltr_m, j - 1) + colwar)
                s2 = SC(lambda s, xslot=xslot, cq=cq: s.activation(
                    out=a_xln[:, :], in_=xslot, func=AF.Square, accum_out=cq),
                    waits=[s1])
                v1 = VE(lambda v, cs=cs, cm_=cm_: v.tensor_scalar(
                    out=cm_, in0=cs, scalar1=1.0 / D, scalar2=None,
                    op0=ALU.mult), waits=[s1] + colwar)
                v2 = VE(lambda v, cq=cq, cvv=cvv: v.tensor_scalar(
                    out=cvv, in0=cq, scalar1=1.0 / D, scalar2=None,
                    op0=ALU.mult), waits=[s2])
                v3 = VE(lambda v, cm_=cm_: v.tensor_tensor(
                    out=a_xln[:, 0:1], in0=cm_, in1=cm_, op=ALU.mult),
                    waits=[v1, s2])
                v4 = VE(lambda v, cvv=cvv: v.tensor_tensor(
                    out=cvv, in0=cvv, in1=a_xln[:, 0:1], op=ALU.subtract),
                    waits=[v2, v3])
                r1 = VE(lambda v, cvv=cvv, ctmp=ctmp: v.tensor_scalar(
                    out=ctmp, in0=cvv, scalar1=1e-5, scalar2=None, op0=ALU.add),
                    waits=[v4])
                r2 = VE(lambda v, ctmp=ctmp, crq=crq: v.reciprocal(
                    out=crq, in_=ctmp), waits=[r1])
                r3 = SC(lambda s, crq=crq, rs_=rs_: s.activation(
                    out=rs_, in_=crq, func=AF.Sqrt), waits=[r2])
                n1 = VE(lambda v, cm_=cm_, rs_=rs_, nb_=nb_: v.tensor_tensor(
                    out=nb_, in0=cm_, in1=rs_, op=ALU.mult), waits=[r3] + colwar)
                n2 = VE(lambda v, nb_=nb_: v.tensor_scalar(
                    out=nb_, in0=nb_, scalar1=-1.0, scalar2=None, op0=ALU.mult),
                    waits=[n1])
                xln_m[j] = SC(lambda s, xslot=xslot, rs_=rs_, nb_=nb_:
                              s.activation(
                    out=a_xln[:, :], in_=xslot, func=AF.Identity, scale=rs_,
                    bias=nb_), waits=[n2, v3])
                for cc in range(8):
                    bank = apC if cc < 4 else apD
                    q = cc % 4
                    xltr_m[j] = TE(lambda t, cc=cc, bank=bank, q=q: t.transpose(
                        out=bank[:, q * 128:(q + 1) * 128],
                        in_=a_xln[:, cc * 128:(cc + 1) * 128],
                        identity=ident_f[:, :]),
                        waits=([xln_m[j]] + mg(xlte_m, j - 1)
                               + mg(smm, (j - 1, 1, 7)))
                        if cc == 0 else [xln_m[j]])
                for cc in range(8):
                    bank = apC if cc < 4 else apD
                    q = cc % 4
                    xlte_m[j] = SC(lambda s, cc=cc, bank=bank, q=q: s.activation(
                        out=a_xlnT[:, cc * 128:(cc + 1) * 128],
                        in_=bank[:, q * 128:(q + 1) * 128], func=AF.Identity),
                        waits=[xltr_m[j]]
                        + (mg(pmm, (j - 1, 4, 1)) if cc == 0 else []))
                for half in range(2):
                    bank = apA if half == 0 else apB
                    pmm[(j, 0, half)] = proj_mm(
                        j, 0, half, bank, mg(pev, (j - 1, 4, half)))
                    pev[(j, 0, half)] = SC(lambda s, half=half, bank=bank:
                                           s.activation(
                        out=a_f[:, half * 512:(half + 1) * 512], in_=bank[:, :],
                        func=AF.Silu),
                        waits=[pmm[(j, 0, half)]]
                        + (mg(fm_m, (j - 1, 2)) if half == 0 else []))
                fm_m[(j, 0)] = fmap_norm(
                    j, a_qn, 0, pev[(j, 0, 1)],
                    mg(qst_m, j - 1) + [pev[(j, 0, 0)]], sl_=0)
                qst_m[j] = GP(lambda g, j=j: g.dma_start(
                    out=qn_hbm[j * 128:(j + 1) * 128, :], in_=a_qn[:, :]),
                    inc=sw, k=16, waits=[fm_m[(j, 0)]])
                for kk in range(2):
                    for half in range(2):
                        bank = apA if half == 0 else apB
                        pmm[(j, 1 + kk, half)] = proj_mm(
                            j, 1 + kk, half, bank,
                            [pev[(j, 2 + kk, half)]] if kk > 0 else
                            [pev[(j, 0, half)]])
                        ksl = (1 + kk) % 2
                        kwar = ([fm_m[(j, 0)]] if kk == 1
                                else mg(fm_m, (j - 1, 1)))
                        pev[(j, 1 + kk, half)] = SC(
                            lambda s, half=half, bank=bank, ksl=ksl:
                            s.activation(
                                out=a_f[:, ksl * D + half * 512:
                                        ksl * D + (half + 1) * 512],
                                in_=bank[:, :], func=AF.Silu),
                            waits=[pmm[(j, 1 + kk, half)]]
                            + (kwar if half == 0 else []))
                    fm_m[(j, 1 + kk)] = fmap_norm(
                        j, a_kn, H, pev[(j, 1 + kk, 1)],
                        mg(smm, (j, kk - 1, 7)) + mg(smm, (j - 1, 1, 7))
                        + [pev[(j, 1 + kk, 0)]], sl_=(1 + kk) % 2)
                    kn_m[(j, kk)] = fm_m[(j, 1 + kk)]
                    for half in range(2):
                        bank = apA if half == 0 else apB
                        pmm[(j, 3 + kk, half)] = proj_mm(
                            j, 3 + kk, half, bank,
                            [pev[(j, 1 + kk, half)]])
                        pev[(j, 3 + kk, half)] = SC(
                            lambda s, half=half, bank=bank, kk=kk: s.activation(
                                out=a_vn[:, half * 512:(half + 1) * 512],
                                in_=bank[:, :], func=AF.Identity),
                            waits=[pmm[(j, 3 + kk, half)],
                                   fm_m[(j, 1 + kk)]]
                            + mg(smm, (j, kk - 1, 7)) + mg(smm, (j - 1, 1, 7)))
                    vn_m[(j, kk)] = pev[(j, 3 + kk, 1)]
                    for hp in range(8):
                        bank = apC if hp % 2 == 0 else apD
                        if hp >= 2:
                            swar = mg(sadd, (j, kk, hp - 2))
                        elif kk > 0:
                            swar = mg(sadd, (j, kk - 1, 6 + hp))
                        else:
                            swar = mg(sadd, (j - 1, 1, 6 + hp))
                        smm[(j, kk, hp)] = TE(lambda t, hp=hp, bank=bank:
                                              t.matmul(
                            bank[:, 0:128], a_kn[:, hp * 128:(hp + 1) * 128],
                            a_vn[:, hp * 128:(hp + 1) * 128],
                            start=True, stop=True),
                            waits=[kn_m[(j, kk)], vn_m[(j, kk)]] + swar
                            + ([xltr_m[j]] if hp == 0 and kk == 0 else []))
                        sadd[(j, kk, hp)] = VE(lambda v, kk=kk, hp=hp,
                                               bank=bank: v.tensor_tensor(
                            out=a_st[:, kk * 1024 + hp * 128:
                                     kk * 1024 + (hp + 1) * 128],
                            in0=a_st[:, kk * 1024 + hp * 128:
                                     kk * 1024 + (hp + 1) * 128],
                            in1=bank[:, 0:128], op=ALU.add),
                            waits=[smm[(j, kk, hp)], (sg.h, stm[1])]
                            + mg(sadd, (j - 1, kk, hp)))

            sf1 = VE(lambda v: v.tensor_tensor(
                out=a_st[:, 0:1024], in0=a_st[:, 0:1024],
                in1=a_st[:, 1024:2048], op=ALU.mult),
                waits=[sadd[(NT - 1, 0, 7)], sadd[(NT - 1, 1, 7)],
                       sadd[(NT - 1, 0, 6)], sadd[(NT - 1, 1, 6)]])
            sf2 = VE(lambda v: v.tensor_scalar(
                out=a_st[:, 0:1024], in0=a_st[:, 0:1024], scalar1=scale2,
                scalar2=None, op0=ALU.mult), waits=[sf1])
            scp = {}
            for h in range(H):
                hp, sub = h // 2, h % 2
                scp[h] = VE(lambda v, hp=hp, sub=sub: v.tensor_copy(
                    out=a_S[sub * 64:sub * 64 + 64, hp * 64:(hp + 1) * 64],
                    in_=a_st[sub * 64:sub * 64 + 64,
                             hp * 128 + sub * 64:hp * 128 + sub * 64 + 64]),
                    waits=[sf2])
            wo_first = SY(lambda s: s.dma_start(
                out=a_w[:, 0:512], in_=wo_t[0, 0, :, :]),
                waits=[pmm[(NT - 1, 4, 1)]])
            for half in range(2):
                for cc in range(8):
                    if half == 0 and cc == 0:
                        continue
                    SY(lambda s, half=half, cc=cc: s.dma_start(
                        out=a_w[:, (half * 8 + cc) * 512:
                                (half * 8 + cc + 1) * 512],
                        in_=wo_t[half, cc, :, :]))
            wo_ready = (sl.h, sl.n)

            for j in range(NT):
                pb = 24 * (j % 2)
                SY(lambda s, j=j: s.dma_start(
                    out=a_qn[:, :], in_=qn_hbm[j * 128:(j + 1) * 128, :]),
                   inc=r_aqn[0], waits=W(st, sv, sw))
                for hp in range(8):
                    TE(lambda t, hp=hp: t.transpose(
                        out=apT[:, hp * 128:(hp + 1) * 128],
                        in_=a_qn[:, hp * 128:(hp + 1) * 128],
                        identity=ident_f[:, :]),
                       waits=(W(st, sv) + W(r_aqn[0])) if hp == 0 else ())
                VE(lambda v: v.tensor_copy(out=a_qT[:, :], in_=apT[:, 0:D]),
                   waits=W(st, sv))
                for h in range(H):
                    bank = apA if h < 8 else apB
                    sub = h % 2
                    TE(lambda t, h=h, bank=bank, sub=sub: t.matmul(
                        bank[:, (h % 8) * 64:(h % 8 + 1) * 64],
                        a_qT[sub * 64:sub * 64 + 64,
                             (h // 2) * 128:(h // 2 + 1) * 128],
                        a_S[sub * 64:sub * 64 + 64, (h // 2) * 64:(h // 2 + 1) * 64],
                        start=True, stop=True), waits=W(sv, st))
                VE(lambda v: v.tensor_copy(out=a_og[:, 0:512], in_=apA[:, :]),
                   waits=W(st, sv))
                VE(lambda v: v.tensor_copy(out=a_og[:, 512:1024], in_=apB[:, :]),
                   waits=W(st, sv))
                cs = stat[:, pb + 0:pb + 1]
                cq = stat[:, pb + 1:pb + 2]
                cm_ = stat[:, pb + 2:pb + 3]
                cv = stat[:, pb + 3:pb + 4]
                ln_stats(a_og[:, :], a_f[:, 0:D], cs, cq, cm_, cv)
                ctmp = stat[:, pb + 4:pb + 5]
                crq = stat[:, pb + 5:pb + 6]
                rs_ = stat[:, pb + 6:pb + 7]
                nb_ = stat[:, pb + 7:pb + 8]
                rsqrt_eps(cv, 1e-5, ctmp, crq, rs_)
                VE(lambda v, nb_=nb_, cm_=cm_, rs_=rs_: v.tensor_tensor(
                    out=nb_, in0=cm_, in1=rs_, op=ALU.mult), waits=W(ss, sv))
                VE(lambda v, nb_=nb_: v.tensor_scalar(out=nb_, in0=nb_, scalar1=-1.0,
                                                      scalar2=None, op0=ALU.mult),
                   waits=W(sv))
                SC(lambda s, rs_=rs_, nb_=nb_: s.activation(
                    out=a_og[:, :], in_=a_og[:, :], func=AF.Identity, scale=rs_,
                    bias=nb_), waits=W(sv, ss))
                VE(lambda v: v.tensor_tensor(out=a_og[:, :], in0=a_og[:, :],
                                             in1=a_lngb[:, :], op=ALU.mult),
                   waits=W(ss, sv, sg))
                for q in range(8):
                    bank = apC if q < 4 else apD
                    TE(lambda t, q=q, bank=bank: t.transpose(
                        out=bank[:, (q % 4) * 128:(q % 4 + 1) * 128],
                        in_=a_og[:, q * 128:(q + 1) * 128],
                        identity=ident_f[:, :]), waits=W(sv, st, ss))
                for q in range(8):
                    bank = apC if q < 4 else apD
                    SC(lambda s, q=q, bank=bank: s.activation(
                        out=a_ogT[:, q * 128:(q + 1) * 128],
                        in_=bank[:, (q % 4) * 128:(q % 4 + 1) * 128],
                        func=AF.Identity), waits=W(st, ss))
                xslot = a_xt[:, (j % 2) * D:(j % 2) * D + D]
                SY(lambda s, j=j, xslot=xslot: s.dma_start(
                    out=xslot, in_=xs2[j * 128:(j + 1) * 128, :]),
                   inc=r_axl[j % 2], waits=W(sv))
                for half in range(2):
                    bank = apA if half == 0 else apB
                    for cc in range(8):
                        TE(lambda t, half=half, cc=cc, bank=bank: t.matmul(
                            bank[:, :], a_ogT[:, cc * 128:(cc + 1) * 128],
                            a_w[:, (half * 8 + cc) * 512:(half * 8 + cc + 1) * 512],
                            start=(cc == 0), stop=False), inc=None,
                           waits=(([wo_ready] + W(ss, sv, st)) if cc == 0 else ()))
                    TE(lambda t, half=half, bank=bank: t.matmul(
                        bank[:, :], ones_b[:, :],
                        bwo[0:1, half * 512:(half + 1) * 512],
                        start=False, stop=True))
                    VE(lambda v, half=half, bank=bank, xslot=xslot, j=j:
                       v.tensor_tensor(
                           out=a_xo[:, half * 512:(half + 1) * 512],
                           in0=bank[:, :],
                           in1=xslot[:, half * 512:(half + 1) * 512], op=ALU.add),
                       waits=W(st, sv, sw) + W(r_axl[j % 2]))
                GP(lambda g, j=j: g.dma_start(
                    out=xs3[j * 128:(j + 1) * 128, :], in_=a_xo[:, :]),
                   inc=sw, k=16, waits=W(sv))
            pa.run()

    if "F2" in phases:
        phase_ffn(xs3, y_out, "f2")

    if debug_out is not None:
        dbg = new_prog()
        srcs = {"xs1": xs1, "xs2": xs2, "xs3": xs3, "qn": qn_hbm}
        dsrc = srcs[debug_out]
        for j in range(NT):
            GP(lambda g, j=j: g.dma_start(
                out=y_out[j * 128:(j + 1) * 128, :],
                in_=dsrc[j * 128:(j + 1) * 128, :]), inc=sw, k=16)
        dbg.run()

    fin = new_prog()
    fin.run()

    es.close()
    return nc


def make_inputs(inputs, T=NSEQ_FULL):
    x = np.ascontiguousarray(np.asarray(inputs["L0_attn_a_hidden_states"],
                                        dtype=np.float32))
    base = {
        "norm_w": inputs["L0_attn_a_norm_w"],
        "W_in": inputs["L0_attn_a_W_in"],
        "conv_w": np.asarray(inputs["L0_attn_a_conv_w"]).reshape(E, KC),
        "W_x": inputs["L0_attn_a_W_x"],
        "W_dt": inputs["L0_attn_a_W_dt"],
        "b_dt": inputs["L0_attn_a_b_dt"],
        "A_log": inputs["L0_attn_a_A_log"],
        "D_param": inputs["L0_attn_a_D_param"],
        "W_out": inputs["L0_attn_a_W_out"],
        "WqT": np.asarray(inputs["L0_attn_b_Wq"]).T,
        "bq": inputs["L0_attn_b_bq"],
        "WkT": np.transpose(np.asarray(inputs["L0_attn_b_Wk"]), (0, 2, 1)),
        "bk": inputs["L0_attn_b_bk"],
        "WvT": np.transpose(np.asarray(inputs["L0_attn_b_Wv"]), (0, 2, 1)),
        "bv": inputs["L0_attn_b_bv"],
        "WoT": np.asarray(inputs["L0_attn_b_Wo"]).T,
        "bo": inputs["L0_attn_b_bo"],
        "ln_g": inputs["L0_attn_b_ln_g"],
        "ln_b": inputs["L0_attn_b_ln_b"],
        "wgT": np.asarray(inputs["L0_ffn_w_gate"]).T,
        "wuT": np.asarray(inputs["L0_ffn_w_up"]).T,
        "wdT": np.asarray(inputs["L0_ffn_w_down"]).T,
    }
    base = {k: np.ascontiguousarray(np.asarray(v, dtype=np.float32))
            for k, v in base.items()}
    return [dict(base, x0=np.ascontiguousarray(x[b, :T])) for b in range(x.shape[0])]


def kernel(**inputs):
    nc = build(NSEQ_FULL)
    in_maps = make_inputs(inputs, NSEQ_FULL)
    res = run_bass_kernel_spmd(nc, in_maps, core_ids=list(range(8)))
    return np.stack([np.asarray(r["y"], dtype=np.float32) for r in res.results],
                    axis=0)
